# version 13 (final)
# baseline (speedup 1.0000x reference)
import contextlib
import numpy as np
import ml_dtypes

import concourse.bass as bass
import concourse.mybir as mybir
from concourse.bass_utils import run_bass_kernel_spmd

F32 = mybir.dt.float32
BF16 = mybir.dt.bfloat16
AF = mybir.ActivationFunctionType
ALU = mybir.AluOpType
AX = mybir.AxisListType

D = 1024
DFF = 4096
NH = 16
NDC = D // 128
NFC = DFF // 128
EPS_RMS = 1e-6
EPS_LN = 1e-5
NEG = -1e30


def build_nc(T=2048, stop_after=None):
    H = T // 2
    TG = min(512, H)
    NTG = H // TG
    NTC = H // 128

    nc = bass.Bass("TRN2", target_bir_lowering=False)

    def din(name, shape, dt=F32):
        return nc.dram_tensor(name, shape, dt, kind="ExternalInput").ap()

    xt = din("xt", [D, T])
    wgvd = din("wgvd", [3 * NDC, 128, D])
    wqk = din("wqk", [16, 128, D])
    wvr = din("wvr", [NDC, 128, D])
    wo_ = din("wo", [NDC, 128, D])
    wgu = din("wgu", [2 * NFC, 128, D])
    wout = din("wout", [NDC, 128, DFF])
    w12 = din("w12", [2 * NFC, 128, D])
    wh1 = din("wh1", [NFC, 128, DFF])
    wh2 = din("wh2", [NFC, 128, DFF])
    w3_ = din("w3", [NDC, 128, DFF])
    vecs = din("vecs", [128, 6 * NDC + 4 * NFC + NDC])
    idenf = din("idenf", [128, 128])
    idenb = din("idenb", [128, 128], BF16)
    onesf = din("onesf", [128, 128])
    masks = din("masks", [128, 512], BF16)

    out = nc.dram_tensor("out", [D, T], F32, kind="ExternalOutput").ap()
    dbg = None
    dbg2 = None
    if stop_after is not None:
        dbg = nc.dram_tensor("dbg", [D, H], F32, kind="ExternalOutput").ap()
        dbg2 = nc.dram_tensor("dbg2", [H, D], F32, kind="ExternalOutput").ap()

    RB = 0
    QB = 16 * H
    KB = QB + 16 * H
    VB = KB + 16 * H
    VEND = VB + 2 * NTC * 1024
    if 16 * H >= 16 * 1024:
        VR = QB
        VREND = VEND
    else:
        VR = VEND
        VREND = VR + 16 * 1024
    HA = QB
    if 16 * H >= 32 * TG:
        HBH = RB
        HBL = QB + 64 * TG
    else:
        HBH = HA + 64 * TG
        HBL = HBH + 32 * TG
    PTB = max(VEND, VREND, HBL + 32 * TG, QB + 64 * TG)
    KPB = PTB + 2048
    VPB = KPB + 2048
    ARN = VPB + 2048
    assert ARN <= 74000, f"arena too big: {ARN}"

    ctx = contextlib.ExitStack()
    E = ctx.enter_context
    with ctx:
        X = [E(nc.sbuf_tensor(f"X{c}", [128, H], F32)) for c in range(NDC)]
        ARE = E(nc.sbuf_tensor("ARE", [128, ARN], BF16))
        stp = [E(nc.sbuf_tensor(f"stp{i}", [128, 512], F32)) for i in range(2)]
        mean_b = E(nc.sbuf_tensor("mean_b", [128, 512], F32))
        rstd_b = E(nc.sbuf_tensor("rstd_b", [128, 512], F32))
        tmp_b = E(nc.sbuf_tensor("tmp_b", [128, 512], F32))
        wstg = [E(nc.sbuf_tensor(f"wstg{i}", [128, 512], F32)) for i in range(2)]
        sp = [E(nc.sbuf_tensor(f"sp{i}", [128, 256], F32)) for i in range(2)]
        scn_c = E(nc.sbuf_tensor("scn_c", [128, NDC], F32))
        nmb = E(nc.sbuf_tensor("nmb", [128, 4], F32))
        smb = E(nc.sbuf_tensor("smb", [128, 4], F32))
        rrb = E(nc.sbuf_tensor("rrb", [128, 4], F32))
        VEC = E(nc.sbuf_tensor("VEC", [128, 6 * NDC + 4 * NFC + NDC], F32))
        c_idf = E(nc.sbuf_tensor("c_idf", [128, 128], F32))
        c_idb = E(nc.sbuf_tensor("c_idb", [128, 128], BF16))
        c_ones = E(nc.sbuf_tensor("c_ones", [128, 128], F32))
        c_mask = E(nc.sbuf_tensor("c_mask", [128, 512], BF16))
        wsl_hi = [E(nc.sbuf_tensor(f"wslh{i}", [128, 512], BF16)) for i in range(4)]
        wsl_lo = [E(nc.sbuf_tensor(f"wsll{i}", [128, 512], BF16)) for i in range(4)]
        tsl = [E(nc.sbuf_tensor(f"tsl{i}", [128, 2048], BF16)) for i in range(2)]
        pb = [E(nc.psum_tensor(f"pb{i}", [128, 512], F32)) for i in range(8)]
        nc._sbuf_left = nc.sbuf_bytes_remaining

        def vv(i0, n):
            return VEC[:, i0:i0 + n]
        c_gmix = vv(0, NDC)
        c_gattn = vv(NDC, NDC)
        c_gffn = vv(2 * NDC, NDC)
        c_bg = vv(3 * NDC, NDC)
        c_bv = vv(4 * NDC, NDC)
        c_bd = vv(5 * NDC, NDC)
        c_s1 = vv(6 * NDC, NFC)
        c_s2 = vv(6 * NDC + NFC, NFC)
        c_sh1 = vv(6 * NDC + 2 * NFC, NFC)
        c_sh2 = vv(6 * NDC + 3 * NFC, NFC)
        c_s3 = vv(6 * NDC + 4 * NFC, NDC)

        SEM = {}
        for s in ("dx", "dw0", "dw1", "dg0", "dg1", "dg2", "dg3", "do",
                  "st", "ss", "sv"):
            SEM[s] = [E(nc.semaphore(s)), 0]

        def sh(s):
            return SEM[s][0]

        def cnt(s):
            return SEM[s][1]

        def inc(instr, s, by=1):
            SEM[s][1] += by
            instr.then_inc(SEM[s][0], by)
            return SEM[s][1]

        def wt(eng, s, val=None):
            v = SEM[s][1] if val is None else val
            if v > 0:
                eng.wait_ge(SEM[s][0], v)

        def av(off, n):
            return ARE[:, off:off + n]

        def r_hi(c):
            return av(RB + c * H, H)

        def r_lo(c):
            return av(RB + (8 + c) * H, H)

        def q_hi(c):
            return av(QB + c * H, H)

        def q_lo(c):
            return av(QB + (8 + c) * H, H)

        def vr_hi(kc):
            return av(VR + kc * 1024, 1024)

        def vr_lo(kc):
            return av(VR + (8 + kc) * 1024, 1024)

        def k_hi(c):
            return av(KB + c * H, H)

        def k_lo(c):
            return av(KB + (8 + c) * H, H)

        def kp_hi(c):
            return av(KPB + c * 128, 128)

        def kp_lo(c):
            return av(KPB + (8 + c) * 128, 128)

        def v_hi(t):
            return av(VB + (t - 1) * 1024, 1024)

        def v_lo(t):
            return av(VB + (NTC + t - 1) * 1024, 1024)

        def vp_hi():
            return av(VPB, 1024)

        def vp_lo():
            return av(VPB + 1024, 1024)

        def x4g_hi(c):
            return av(c * TG, TG)

        def x4g_lo(c):
            return av((8 + c) * TG, TG)

        def ha_hi(f):
            return av(HA + f * TG, TG)

        def ha_lo(f):
            return av(HA + (32 + f) * TG, TG)

        def hb_hi(f):
            return av(HBH + f * TG, TG)

        def hb_lo(f):
            return av(HBL + f * TG, TG)

        def pt_hi(slot):
            return av(PTB + slot * 512, 256)

        def pt_lo(slot):
            return av(PTB + slot * 512 + 256, 256)

        with nc.Block() as blk:
            @blk.sync
            def _(sync):
                for dst, src in [(VEC, vecs), (c_idf, idenf), (c_idb, idenb),
                                 (c_ones, onesf), (c_mask, masks)]:
                    inc(sync.dma_start(out=dst[:], in_=src), "dx", 16)

            @blk.vector
            def _(v):
                for c in range(NDC):
                    v.memset(kp_hi(c), 0.0)
                    v.memset(kp_lo(c), 0.0)
                v.memset(vp_hi(), 0.0)
                v.memset(vp_lo(), 0.0)
                v.memset(scn_c[:], 0.0)
        CONST_DX = cnt("dx")

        class FStream:

            def __init__(self, blk, srcs):
                self.blk = blk
                self.srcs = srcs
                self.split = {}
                self.used = {}
                self.free0 = [("ss", cnt("ss")), ("sv", cnt("sv")),
                              ("st", cnt("st"))]
                self.j = 0
                for _ in range(4):
                    self.put()

            def put(self):
                j = self.j
                if j >= len(self.srcs):
                    return
                self.j += 1
                src = self.srcs[j]
                blk = self.blk
                kl = src.shape[-1]
                par, q = j % 2, j % 4
                if j >= 2:
                    fdma = [("sv", self.split[j - 2][1])]
                else:
                    fdma = self.free0
                fwsl = [("st", self.used[j - 4])] if j >= 4 else self.free0

                @blk.sync
                def _(sync):
                    for s_, v_ in fdma:
                        wt(sync, s_, v_)
                    inc(sync.dma_start(out=wstg[par][:, 0:kl], in_=src),
                        "dw%d" % par, 16)
                dmk = cnt("dw%d" % par)

                @blk.vector
                def _(v):
                    v.wait_ge(sh("dw%d" % par), dmk)
                    for s_, v_ in fwsl:
                        wt(v, s_, v_)
                    inc(v.tensor_copy(wsl_hi[q][:, 0:kl], wstg[par][:, 0:kl]),
                        "sv")
                hi_mk = cnt("sv")

                @blk.vector
                def _(v):
                    v.wait_ge(sh("sv"), hi_mk)
                    inc(v.scalar_tensor_tensor(
                        wsl_lo[q][:, 0:kl], wstg[par][:, 0:kl], 1.0,
                        wsl_hi[q][:, 0:kl], op0=ALU.mult, op1=ALU.subtract),
                        "sv")
                self.split[j] = (hi_mk, cnt("sv"), q)

            def twait(self, t, j):
                t.wait_ge(sh("sv"), self.split[j][1])

            def buf(self, j):
                return self.split[j][2]

            def done(self, j):
                self.used[j] = cnt("st")
                self.put()

        def mm3h(t, psum, q, rhs_hi, rhs_lo, kbase, nkc, first, last):
            mm = None
            for kc in range(nkc):
                hi = wsl_hi[q][:, kc * 128:(kc + 1) * 128]
                lo = wsl_lo[q][:, kc * 128:(kc + 1) * 128]
                t.matmul(psum, hi, rhs_hi(kbase + kc),
                         start=(first and kc == 0), stop=False)
                t.matmul(psum, hi, rhs_lo(kbase + kc), start=False, stop=False)
                mm = t.matmul(psum, lo, rhs_hi(kbase + kc), start=False,
                              stop=(last and kc == nkc - 1))
            return mm

        def norm_rms(blk, gamma, xmk=None, rfree=None, entry_sv=None):
            rdone = []
            dx_all = cnt("dx")
            e_sv = cnt("sv") if entry_sv is None else entry_sv
            e_ss = cnt("ss")
            sq_mk = [0, 0]
            mm_mk = [0, 0]
            pb6_free = 0
            tmp_free = 0
            rstd_free = 0
            hi_mk = [0, 0]
            lo_mk = [0, 0]
            for g in range(NTG):
                gb = g * TG
                for c in range(NDC):
                    @blk.scalar
                    def _(s, c=c, g=g):
                        if g == 0 and c == 0:
                            s.wait_ge(sh("dx"), dx_all)
                            wt(s, "sv", e_sv)
                        if xmk is not None:
                            for s_, v_ in xmk(c, g):
                                wt(s, s_, v_)
                        wt(s, "st", mm_mk[c % 2])
                        wt(s, "sv", lo_mk[c % 2])
                        inc(s.activation(stp[c % 2][:, 0:TG],
                                         X[c][:, gb:gb + TG], AF.Square), "ss")
                    sq_mk[c % 2] = cnt("ss")

                    @blk.tensor
                    def _(t, c=c, g=g, pv=pb6_free):
                        t.wait_ge(sh("ss"), sq_mk[c % 2])
                        if c == 0:
                            wt(t, "sv", pv)
                        inc(t.matmul(pb[6][:, 0:TG], c_ones[:],
                                     stp[c % 2][:, 0:TG], start=(c == 0),
                                     stop=(c == NDC - 1)), "st")
                    mm_mk[c % 2] = cnt("st")
                done_mk = cnt("st")

                @blk.vector
                def _(v, g=g, tf=tmp_free):
                    v.wait_ge(sh("st"), done_mk)
                    wt(v, "ss", tf)
                    if g == 0:
                        wt(v, "ss", e_ss)
                    inc(v.tensor_scalar(tmp_b[:, 0:TG], pb[6][:, 0:TG],
                                        1.0 / D, EPS_RMS, op0=ALU.mult,
                                        op1=ALU.add), "sv")
                ts_mk = cnt("sv")
                pb6_free = ts_mk

                @blk.vector
                def _(v, g=g):
                    v.wait_ge(sh("sv"), ts_mk)
                    inc(v.reciprocal(tmp_b[:, 0:TG], tmp_b[:, 0:TG]), "sv")
                rc_mk = cnt("sv")

                @blk.scalar
                def _(s, g=g, rf=rstd_free):
                    s.wait_ge(sh("sv"), rc_mk)
                    wt(s, "sv", rf)
                    inc(s.activation(rstd_b[:, 0:TG], tmp_b[:, 0:TG],
                                     AF.Sqrt), "ss")
                sqrt_mk = cnt("ss")
                tmp_free = sqrt_mk

                for c in range(NDC):
                    @blk.vector
                    def _(v, c=c, g=g):
                        if c == 0:
                            v.wait_ge(sh("ss"), sqrt_mk)
                        if g == 0 and c == 0 and rfree is not None:
                            wt(v, rfree[0], rfree[1])
                        wt(v, "sv", lo_mk[c % 2])
                        inc(v.scalar_tensor_tensor(
                            stp[c % 2][:, 0:TG], X[c][:, gb:gb + TG],
                            gamma[:, c:c + 1], rstd_b[:, 0:TG],
                            op0=ALU.mult, op1=ALU.mult), "sv")
                    ap_mk = cnt("sv")

                    @blk.vector
                    def _(v, c=c, g=g, am=ap_mk):
                        v.wait_ge(sh("sv"), am)
                        inc(v.tensor_copy(r_hi(c)[:, gb:gb + TG],
                                          stp[c % 2][:, 0:TG]), "sv")
                    hi_mk[c % 2] = cnt("sv")

                    @blk.vector
                    def _(v, c=c, g=g, im=hi_mk[c % 2]):
                        v.wait_ge(sh("sv"), im)
                        inc(v.scalar_tensor_tensor(
                            r_lo(c)[:, gb:gb + TG], stp[c % 2][:, 0:TG], 1.0,
                            r_hi(c)[:, gb:gb + TG], op0=ALU.mult,
                            op1=ALU.subtract), "sv")
                    lo_mk[c % 2] = cnt("sv")
                rstd_free = cnt("sv")
                rdone.append(cnt("sv"))
            return rdone

        def norm_ln(blk, xmk=None, entry_sv=None):
            adone = []
            dx_all = cnt("dx")
            e_sv = cnt("sv") if entry_sv is None else entry_sv
            e_ss = cnt("ss")
            sq_mk = [0, 0]
            mm_mk = [0, 0]
            pb6_free = 0
            pb7_free = 0
            tmp_free = 0
            mean_free = 0
            rstd_free = 0
            a1_mk = [0, 0]
            a2_mk = [0, 0]
            for g in range(NTG):
                gb = g * TG
                for c in range(NDC):
                    @blk.scalar
                    def _(s, c=c, g=g):
                        if g == 0 and c == 0:
                            s.wait_ge(sh("dx"), dx_all)
                            wt(s, "sv", e_sv)
                        if xmk is not None:
                            for s_, v_ in xmk(c, g):
                                wt(s, s_, v_)
                        wt(s, "st", mm_mk[c % 2])
                        wt(s, "sv", a2_mk[c % 2])
                        inc(s.activation(stp[c % 2][:, 0:TG],
                                         X[c][:, gb:gb + TG], AF.Square), "ss")
                    sq_mk[c % 2] = cnt("ss")

                    @blk.tensor
                    def _(t, c=c, g=g, pv=pb6_free):
                        t.wait_ge(sh("ss"), sq_mk[c % 2])
                        if c == 0:
                            wt(t, "sv", pv)
                        inc(t.matmul(pb[6][:, 0:TG], c_ones[:],
                                     stp[c % 2][:, 0:TG], start=(c == 0),
                                     stop=(c == NDC - 1)), "st")
                    mm_mk[c % 2] = cnt("st")

                @blk.tensor
                def _(t, g=g, pm=pb7_free):
                    wt(t, "sv", pm)
                    if xmk is not None:
                        for c in range(NDC):
                            for s_, v_ in xmk(c, g):
                                wt(t, s_, v_)
                    mm = None
                    for c in range(NDC):
                        mm = t.matmul(pb[7][:, 0:TG], c_ones[:],
                                      X[c][:, gb:gb + TG], start=(c == 0),
                                      stop=(c == NDC - 1))
                    inc(mm, "st")
                done_mk = cnt("st")

                @blk.vector
                def _(v, g=g, mf=mean_free):
                    v.wait_ge(sh("st"), done_mk)
                    wt(v, "sv", mf)
                    inc(v.tensor_scalar(mean_b[:, 0:TG], pb[7][:, 0:TG],
                                        1.0 / D, None, op0=ALU.mult), "sv")
                mn_mk = cnt("sv")
                pb7_free = mn_mk

                @blk.scalar
                def _(s, g=g, tf=tmp_free):
                    s.wait_ge(sh("sv"), mn_mk)
                    wt(s, "ss", tf)
                    if g == 0:
                        wt(s, "ss", e_ss)
                    inc(s.activation(tmp_b[:, 0:TG], mean_b[:, 0:TG],
                                     AF.Square), "ss")
                m2_mk = cnt("ss")

                @blk.vector
                def _(v, g=g):
                    v.wait_ge(sh("ss"), m2_mk)
                    inc(v.scalar_tensor_tensor(
                        tmp_b[:, 0:TG], pb[6][:, 0:TG], 1.0 / D,
                        tmp_b[:, 0:TG], op0=ALU.mult, op1=ALU.subtract), "sv")
                ve_mk = cnt("sv")
                pb6_free = ve_mk

                @blk.vector
                def _(v, g=g):
                    v.wait_ge(sh("sv"), ve_mk)
                    inc(v.tensor_scalar(tmp_b[:, 0:TG], tmp_b[:, 0:TG], 1.0,
                                        EPS_LN, op0=ALU.mult, op1=ALU.add),
                        "sv")
                ep_mk = cnt("sv")

                @blk.vector
                def _(v, g=g):
                    v.wait_ge(sh("sv"), ep_mk)
                    inc(v.reciprocal(tmp_b[:, 0:TG], tmp_b[:, 0:TG]), "sv")
                rc_mk = cnt("sv")

                @blk.scalar
                def _(s, g=g, rf=rstd_free):
                    s.wait_ge(sh("sv"), rc_mk)
                    wt(s, "sv", rf)
                    inc(s.activation(rstd_b[:, 0:TG], tmp_b[:, 0:TG],
                                     AF.Sqrt), "ss")
                sqrt_mk = cnt("ss")
                tmp_free = sqrt_mk

                for c in range(NDC):
                    @blk.vector
                    def _(v, c=c, g=g):
                        if c == 0:
                            v.wait_ge(sh("ss"), sqrt_mk)
                        wt(v, "sv", a2_mk[c % 2])
                        inc(v.scalar_tensor_tensor(
                            stp[c % 2][:, 0:TG], X[c][:, gb:gb + TG], 1.0,
                            mean_b[:, 0:TG], op0=ALU.mult,
                            op1=ALU.subtract), "sv")
                    a1_mk[c % 2] = cnt("sv")

                    @blk.vector
                    def _(v, c=c, g=g, am=a1_mk[c % 2]):
                        v.wait_ge(sh("sv"), am)
                        inc(v.tensor_tensor(X[c][:, gb:gb + TG],
                                            stp[c % 2][:, 0:TG],
                                            rstd_b[:, 0:TG], op=ALU.mult),
                            "sv")
                    a2_mk[c % 2] = cnt("sv")
                mean_free = cnt("sv")
                rstd_free = cnt("sv")
                adone.append(cnt("sv"))
            return adone

        def emit_loadx(blk, h):
            @blk.sync
            def _(sync):
                wt(sync, "do")
                for c in range(NDC):
                    inc(sync.dma_start(
                        out=X[c][:],
                        in_=xt[c * 128:(c + 1) * 128, h * H:(h + 1) * H]),
                        "dx", 16)

        def emit_mixmm(blk, h, rdone):
            srcs = []
            for oc in range(NDC):
                for wi in range(3):
                    j = oc * 3 + wi
                    srcs.append(wgvd[j][:, 0:512])
                    srcs.append(wgvd[j][:, 512:1024])
            fs = FStream(blk, srcs)
            bank_mk = {}
            h_strip = [stp[0], stp[1]]
            scan_mk = [0, 0]
            add_mk = [rdone[-1], rdone[-1]]
            ep_mk = rdone[-1]
            xadd = {}
            for oc in range(NDC):
                for wi in range(3):
                    ju = (oc * 3 + wi) * 2
                    for g in range(NTG):
                        bank = pb[wi + 3 * (g % 2)]

                        @blk.tensor
                        def _(t, oc=oc, g=g, wi=wi, ju=ju, bank=bank):
                            fs.twait(t, ju)
                            t.wait_ge(sh("sv"), rdone[g])
                            if (wi, g % 2) in bank_mk:
                                t.wait_ge(sh("ss"), bank_mk[(wi, g % 2)])
                            mm3h(t, bank[:, 0:TG], fs.buf(ju),
                                 lambda kc: r_hi(kc)[:, g * TG:(g + 1) * TG],
                                 lambda kc: r_lo(kc)[:, g * TG:(g + 1) * TG],
                                 0, 4, True, False)
                            fs.twait(t, ju + 1)
                            mm = mm3h(t, bank[:, 0:TG], fs.buf(ju + 1),
                                      lambda kc: r_hi(kc)[:, g * TG:(g + 1) * TG],
                                      lambda kc: r_lo(kc)[:, g * TG:(g + 1) * TG],
                                      4, 4, False, True)
                            inc(mm, "st")
                    fs.done(ju)
                    fs.done(ju + 1)
                mm_last = cnt("st")

                for g in range(NTG):
                    gb = g * TG
                    pg_ = pb[0 + 3 * (g % 2)]
                    pv_ = pb[1 + 3 * (g % 2)]
                    pd_ = pb[2 + 3 * (g % 2)]

                    @blk.scalar
                    def _(s, oc=oc, g=g, pg_=pg_, pv_=pv_, pd_=pd_,
                          em=ep_mk, ml=mm_last):
                        s.wait_ge(sh("st"), ml)
                        wt(s, "sv", em)
                        inc(s.activation(rstd_b[:, 0:TG], pg_[:, 0:TG],
                                         AF.Sigmoid,
                                         bias=c_bg[:, oc:oc + 1]), "ss")
                        inc(s.activation(mean_b[:, 0:TG], pv_[:, 0:TG],
                                         AF.Tanh,
                                         bias=c_bv[:, oc:oc + 1]), "ss")
                        inc(s.activation(tmp_b[:, 0:TG], pd_[:, 0:TG],
                                         AF.Sigmoid,
                                         bias=c_bd[:, oc:oc + 1]), "ss")
                    act_mk = cnt("ss")
                    bank_mk[(0, g % 2)] = act_mk
                    bank_mk[(1, g % 2)] = act_mk
                    bank_mk[(2, g % 2)] = act_mk

                    @blk.vector
                    def _(v, oc=oc, g=g, am=act_mk, em=ep_mk):
                        v.wait_ge(sh("ss"), am)
                        wt(v, "sv", em)
                        inc(v.tensor_tensor(rstd_b[:, 0:TG], rstd_b[:, 0:TG],
                                            mean_b[:, 0:TG], op=ALU.mult),
                            "sv")
                        inc(v.tensor_scalar(tmp_b[:, 0:TG], tmp_b[:, 0:TG],
                                            0.998, 0.001, op0=ALU.mult,
                                            op1=ALU.add), "sv")
                    xsa_mk = cnt("sv")

                    @blk.vector
                    def _(v, oc=oc, g=g, xm=xsa_mk):
                        v.wait_ge(sh("sv"), xm)
                        wt(v, "sv", add_mk[g % 2])
                        if g == 0:
                            init = scn_c[:, oc:oc + 1]
                        else:
                            init = h_strip[(g - 1) % 2][:, TG - 1:TG]
                        inc(v.tensor_tensor_scan(
                            h_strip[g % 2][:, 0:TG], tmp_b[:, 0:TG],
                            rstd_b[:, 0:TG], init, op0=ALU.mult,
                            op1=ALU.add), "sv")
                    scan_mk[g % 2] = cnt("sv")
                    ep_mk = scan_mk[g % 2]

                    @blk.vector
                    def _(v, oc=oc, g=g, gb=gb, sm=scan_mk[g % 2]):
                        v.wait_ge(sh("sv"), sm)
                        inc(v.tensor_tensor(X[oc][:, gb:gb + TG],
                                            X[oc][:, gb:gb + TG],
                                            h_strip[g % 2][:, 0:TG],
                                            op=ALU.add), "sv")
                        if g == NTG - 1:
                            inc(v.tensor_copy(
                                scn_c[:, oc:oc + 1],
                                h_strip[g % 2][:, TG - 1:TG]), "sv")
                    add_mk[g % 2] = cnt("sv")
                    xadd[(oc, g)] = add_mk[g % 2]
            return xadd, cnt("st")

        def emit_atv(blk, h):
            vr_mk = []
            for kc in range(NDC):
                for hw in range(2):
                    j = kc * 2 + hw
                    free = [("sv", vr_mk[j - 2][1])] if j >= 2 else []
                    kl = 512

                    @blk.sync
                    def _(sync, kc=kc, hw=hw, free=free):
                        for s_, v_ in free:
                            wt(sync, s_, v_)
                        inc(sync.dma_start(
                            out=wstg[j % 2][:, 0:512],
                            in_=wvr[kc][:, hw * 512:(hw + 1) * 512]),
                            "dw%d" % (j % 2), 16)
                    dmk = cnt("dw%d" % (j % 2))

                    @blk.vector
                    def _(v, kc=kc, hw=hw, j=j, dmk=dmk):
                        v.wait_ge(sh("dw%d" % (j % 2)), dmk)
                        inc(v.tensor_copy(
                            vr_hi(kc)[:, hw * 512:(hw + 1) * 512],
                            wstg[j % 2][:, 0:512]), "sv")
                    hi_mk = cnt("sv")

                    @blk.vector
                    def _(v, kc=kc, hw=hw, j=j, hm=hi_mk):
                        v.wait_ge(sh("sv"), hm)
                        inc(v.scalar_tensor_tensor(
                            vr_lo(kc)[:, hw * 512:(hw + 1) * 512],
                            wstg[j % 2][:, 0:512], 1.0,
                            vr_hi(kc)[:, hw * 512:(hw + 1) * 512],
                            op0=ALU.mult, op1=ALU.subtract), "sv")
                    vr_mk.append((hi_mk, cnt("sv")))

            vr_v = vr_mk[-1][1]
            ev_mk = {}
            smk_d = {}
            for tc in range(NTC):
                for vh in range(2):
                    u = tc * 2 + vh
                    bank = pb[u % 4]

                    @blk.tensor
                    def _(t, tc=tc, vh=vh, u=u, bank=bank):
                        t.wait_ge(sh("sv"), vr_v)
                        if u >= 4:
                            t.wait_ge(sh("sv"), ev_mk[u - 4])
                        mm = None
                        for kc in range(NDC):
                            lh = r_hi(kc)[:, tc * 128:(tc + 1) * 128]
                            ll = r_lo(kc)[:, tc * 128:(tc + 1) * 128]
                            rh = vr_hi(kc)[:, vh * 512:(vh + 1) * 512]
                            rl = vr_lo(kc)[:, vh * 512:(vh + 1) * 512]
                            t.matmul(bank[:], lh, rh, start=(kc == 0),
                                     stop=False)
                            t.matmul(bank[:], lh, rl, start=False,
                                     stop=False)
                            mm = t.matmul(bank[:], ll, rh, start=False,
                                          stop=(kc == NDC - 1))
                        inc(mm, "st")
                    t_mk = cnt("st")

                    @blk.vector
                    def _(v, tc=tc, vh=vh, bank=bank, tm=t_mk, u=u):
                        v.wait_ge(sh("st"), tm)
                        inc(v.tensor_copy(
                            v_hi(tc + 1)[:, vh * 512:(vh + 1) * 512],
                            bank[:]), "sv")
                    s_mk = cnt("sv")
                    smk_d[u] = s_mk

                    @blk.vector
                    def _(v, tc=tc, vh=vh, u=u, sm=s_mk, bank=bank):
                        v.wait_ge(sh("sv"), sm)
                        inc(v.scalar_tensor_tensor(
                            v_lo(tc + 1)[:, vh * 512:(vh + 1) * 512],
                            bank[:], 1.0,
                            v_hi(tc + 1)[:, vh * 512:(vh + 1) * 512],
                            op0=ALU.mult, op1=ALU.subtract), "sv")
                    ev_mk[u] = cnt("sv")


        def emit_qk(blk, h, part, qfree=None):
            srcs = []
            for oc in range(NDC):
                srcs.append(wqk[part * 8 + oc][:, 0:512])
                srcs.append(wqk[part * 8 + oc][:, 512:1024])
            fs = FStream(blk, srcs)
            scale = 0.125 if part == 0 else 1.0
            ev_mk = {}
            smk_d = {}
            for oc in range(NDC):
                for g in range(NTG):
                    u = oc * NTG + g
                    bank = pb[u % 4]

                    @blk.tensor
                    def _(t, oc=oc, g=g, u=u, bank=bank):
                        fs.twait(t, 2 * oc)
                        if u >= 4:
                            t.wait_ge(sh("sv"), ev_mk[u - 4])
                        mm3h(t, bank[:, 0:TG], fs.buf(2 * oc),
                             lambda kc: r_hi(kc)[:, g * TG:(g + 1) * TG],
                             lambda kc: r_lo(kc)[:, g * TG:(g + 1) * TG],
                             0, 4, True, False)
                        fs.twait(t, 2 * oc + 1)
                        mm = mm3h(t, bank[:, 0:TG], fs.buf(2 * oc + 1),
                                  lambda kc: r_hi(kc)[:, g * TG:(g + 1) * TG],
                                  lambda kc: r_lo(kc)[:, g * TG:(g + 1) * TG],
                                  4, 4, False, True)
                        inc(mm, "st")
                    if g == NTG - 1:
                        fs.done(2 * oc)
                        fs.done(2 * oc + 1)
                    t_mk = cnt("st")

                    gb = g * TG
                    if part == 0:
                        dhi = q_hi(oc)[:, gb:gb + TG]
                        dlo = q_lo(oc)[:, gb:gb + TG]
                    else:
                        dhi = k_hi(oc)[:, gb:gb + TG]
                        dlo = k_lo(oc)[:, gb:gb + TG]

                    @blk.vector
                    def _(v, bank=bank, dhi=dhi, u=u, tm=t_mk):
                        v.wait_ge(sh("st"), tm)
                        inc(v.tensor_scalar(dhi, bank[:, 0:TG],
                                            scale, None,
                                            op0=ALU.mult), "sv")
                    s_mk = cnt("sv")
                    smk_d[u] = s_mk

                    @blk.vector
                    def _(v, u=u, dhi=dhi, dlo=dlo, sm=s_mk,
                          bank=bank):
                        v.wait_ge(sh("sv"), sm)
                        inc(v.scalar_tensor_tensor(
                            dlo, bank[:, 0:TG], scale, dhi,
                            op0=ALU.mult, op1=ALU.subtract), "sv")
                    ev_mk[u] = cnt("sv")

        def emit_atloop(blk, h):
            entry_sv = cnt("sv")
            entry_st = cnt("st")
            its = [(tc, hp) for tc in range(NTC) for hp in range(NH // 2)]
            NIT = len(its)
            t_sc = {}
            v_nm = {}
            s_exp = {}
            v_pn = {}
            t_pt = {}
            s_pt = {}
            v_ptl = {}
            t_av = {}
            s_ao = {}
            v_aol = {}
            sps = [stp[0], stp[1], mean_b]

            def pt2_hi(slot):
                return av(PTB + slot * 1024, 512)

            def pt2_lo(slot):
                return av(PTB + slot * 1024 + 512, 512)

            def stage1(it):
                tc, hp = its[it]
                qc = hp
                sl = it % 2
                pS = pb[sl]
                spv = sps[it % 3]
                moff = 0 if (h == 0 and tc == 0) else 256

                @blk.tensor
                def _(t, tc=tc, qc=qc, pS=pS, moff=moff, it=it):
                    wt(t, "dx")
                    if it == 0:
                        t.wait_ge(sh("sv"), entry_sv)
                    if it >= 2:
                        t.wait_ge(sh("sv"), v_nm[it - 2])
                        t.wait_ge(sh("ss"), s_exp[it - 2])
                    for hx in range(2):
                        po = 64 * hx
                        n0 = 256 * hx
                        t.matmul(pS[:, n0:n0 + 256], c_idb[:],
                                 c_mask[:, moff:moff + 256],
                                 start=True, stop=False)
                        qh = q_hi(qc)[po:po + 64, tc * 128:(tc + 1) * 128]
                        ql = q_lo(qc)[po:po + 64, tc * 128:(tc + 1) * 128]
                        if tc == 0:
                            kparts = [(pS[:, n0:n0 + 128],
                                       kp_hi(qc)[po:po + 64, :],
                                       kp_lo(qc)[po:po + 64, :]),
                                      (pS[:, n0 + 128:n0 + 256],
                                       k_hi(qc)[po:po + 64, 0:128],
                                       k_lo(qc)[po:po + 64, 0:128])]
                        else:
                            kparts = [(pS[:, n0:n0 + 256],
                                       k_hi(qc)[po:po + 64,
                                                (tc - 1) * 128:(tc + 1) * 128],
                                       k_lo(qc)[po:po + 64,
                                                (tc - 1) * 128:(tc + 1) * 128])]
                        mm = None
                        for pi, (pdst, kh, kl) in enumerate(kparts):
                            last = (pi == len(kparts) - 1)
                            t.matmul(pdst, qh, kh, start=False, stop=False)
                            t.matmul(pdst, qh, kl, start=False, stop=False)
                            mm = t.matmul(pdst, ql, kh, start=False,
                                          stop=last)
                    inc(mm, "st")
                t_sc[it] = cnt("st")

                @blk.vector
                def _(v, sl=sl, pS=pS, it=it):
                    v.wait_ge(sh("st"), t_sc[it])
                    inc(v.reduce_max(nmb[:, 2 * sl:2 * sl + 1],
                                     pS[:, 0:256], axis=AX.X,
                                     negate=True), "sv")
                    inc(v.reduce_max(nmb[:, 2 * sl + 1:2 * sl + 2],
                                     pS[:, 256:512], axis=AX.X,
                                     negate=True), "sv")
                v_nm[it] = cnt("sv")

                @blk.scalar
                def _(s, sl=sl, pS=pS, spv=spv, it=it):
                    s.wait_ge(sh("sv"), v_nm[it])
                    if it >= 3:
                        s.wait_ge(sh("sv"), v_pn[it - 3])
                        s.wait_ge(sh("st"), t_pt[it - 3])
                    inc(s.activation(spv[:, 0:256], pS[:, 0:256], AF.Exp,
                                     bias=nmb[:, 2 * sl:2 * sl + 1]),
                        "ss")
                    inc(s.activation(spv[:, 256:512], pS[:, 256:512],
                                     AF.Exp,
                                     bias=nmb[:, 2 * sl + 1:2 * sl + 2]),
                        "ss")
                s_exp[it] = cnt("ss")

                @blk.vector
                def _(v, sl=sl, spv=spv, it=it):
                    v.wait_ge(sh("ss"), s_exp[it])
                    inc(v.reduce_sum(smb[:, 2 * sl:2 * sl + 1],
                                     spv[:, 0:256], axis=AX.X), "sv")
                    inc(v.reduce_sum(smb[:, 2 * sl + 1:2 * sl + 2],
                                     spv[:, 256:512], axis=AX.X), "sv")
                sm_ = cnt("sv")

                @blk.vector
                def _(v, sl=sl, it=it, sm_=sm_):
                    v.wait_ge(sh("sv"), sm_)
                    inc(v.reciprocal(rrb[:, 2 * sl:2 * sl + 2],
                                     smb[:, 2 * sl:2 * sl + 2]), "sv")
                rc = cnt("sv")

                @blk.vector
                def _(v, sl=sl, spv=spv, it=it, rc=rc):
                    v.wait_ge(sh("sv"), rc)
                    inc(v.tensor_scalar(spv[:, 0:256], spv[:, 0:256],
                                        rrb[:, 2 * sl:2 * sl + 1], None,
                                        op0=ALU.mult), "sv")
                    inc(v.tensor_scalar(spv[:, 256:512], spv[:, 256:512],
                                        rrb[:, 2 * sl + 1:2 * sl + 2],
                                        None, op0=ALU.mult), "sv")
                v_pn[it] = cnt("sv")

            def stage2(it):
                sl = it % 2
                pPT = pb[2 + sl]
                spv = sps[it % 3]

                @blk.tensor
                def _(t, pPT=pPT, spv=spv, it=it):
                    t.wait_ge(sh("sv"), v_pn[it])
                    if it >= 2:
                        t.wait_ge(sh("sv"), v_ptl[it - 2])
                    for q4 in range(4):
                        mm = t.transpose(pPT[:, q4 * 128:(q4 + 1) * 128],
                                         spv[:, q4 * 128:(q4 + 1) * 128],
                                         c_idf[:])
                    inc(mm, "st")
                t_pt[it] = cnt("st")

                @blk.vector
                def _(v, sl=sl, pPT=pPT, it=it):
                    v.wait_ge(sh("st"), t_pt[it])
                    if it >= 2:
                        v.wait_ge(sh("st"), t_av[it - 2])
                    inc(v.tensor_copy(pt2_hi(sl), pPT[:, 0:512]), "sv")
                s_pt[it] = cnt("sv")

                @blk.vector
                def _(v, sl=sl, pPT=pPT, it=it):
                    v.wait_ge(sh("sv"), s_pt[it])
                    inc(v.scalar_tensor_tensor(
                        pt2_lo(sl), pPT[:, 0:512], 1.0, pt2_hi(sl),
                        op0=ALU.mult, op1=ALU.subtract), "sv")
                v_ptl[it] = cnt("sv")

            def stage3(it):
                tc, hp = its[it]
                qc = hp
                sl = it % 2
                pA = pb[4 + sl]

                @blk.tensor
                def _(t, tc=tc, hp=hp, sl=sl, pA=pA, it=it):
                    t.wait_ge(sh("sv"), v_ptl[it])
                    if it >= 2:
                        t.wait_ge(sh("sv"), v_aol[it - 2])
                    mm = None
                    for hx in range(2):
                        hh = 2 * hp + hx
                        po = 64 * hx
                        n0 = 256 * hx
                        tp = (0, 64) if po else None
                        for j, vt in ((0, tc), (1, tc + 1)):
                            if vt == 0:
                                vh_ = vp_hi()[:, hh * 64:(hh + 1) * 64]
                                vl_ = vp_lo()[:, hh * 64:(hh + 1) * 64]
                            else:
                                vh_ = v_hi(vt)[:, hh * 64:(hh + 1) * 64]
                                vl_ = v_lo(vt)[:, hh * 64:(hh + 1) * 64]
                            ph = pt2_hi(sl)[:, n0 + j * 128:n0 + (j + 1) * 128]
                            pl = pt2_lo(sl)[:, n0 + j * 128:n0 + (j + 1) * 128]
                            t.matmul(pA[po:po + 64, 0:128], vh_, ph,
                                     start=(j == 0), stop=False,
                                     tile_position=tp)
                            t.matmul(pA[po:po + 64, 0:128], vh_, pl,
                                     start=False, stop=False,
                                     tile_position=tp)
                            mm = t.matmul(pA[po:po + 64, 0:128], vl_, ph,
                                          start=False, stop=(j == 1),
                                          tile_position=tp)
                    inc(mm, "st")
                t_av[it] = cnt("st")

                @blk.vector
                def _(v, tc=tc, qc=qc, pA=pA, it=it):
                    v.wait_ge(sh("st"), t_av[it])
                    if it == 0:
                        v.wait_ge(sh("st"), entry_st)
                    if it >= 2:
                        v.wait_ge(sh("sv"), v_aol[it - 2])
                    dhi = r_hi(qc)[:, tc * 128:(tc + 1) * 128]
                    inc(v.tensor_copy(dhi, pA[:, 0:128]), "sv")
                s_ao[it] = cnt("sv")

                @blk.vector
                def _(v, tc=tc, qc=qc, pA=pA, it=it):
                    v.wait_ge(sh("sv"), s_ao[it])
                    dhi = r_hi(qc)[:, tc * 128:(tc + 1) * 128]
                    dlo = r_lo(qc)[:, tc * 128:(tc + 1) * 128]
                    inc(v.scalar_tensor_tensor(
                        dlo, pA[:, 0:128], 1.0, dhi,
                        op0=ALU.mult, op1=ALU.subtract), "sv")
                v_aol[it] = cnt("sv")

            for step in range(NIT + 3):
                if step < NIT:
                    stage1(step)
                if 0 <= step - 2 < NIT:
                    stage2(step - 2)
                if 0 <= step - 3 < NIT:
                    stage3(step - 3)

            @blk.vector
            def _(v):
                wt(v, "st")
                for c in range(NDC):
                    inc(v.tensor_copy(kp_hi(c),
                                      k_hi(c)[:, H - 128:H]), "sv")
                    inc(v.tensor_copy(kp_lo(c),
                                      k_lo(c)[:, H - 128:H]), "sv")
                inc(v.tensor_copy(vp_hi(), v_hi(NTC)), "sv")
                inc(v.tensor_copy(vp_lo(), v_lo(NTC)), "sv")
            tcs_per_g = TG // 128
            aog = []
            for g in range(NTG):
                last_it = ((g + 1) * tcs_per_g - 1) * (NH // 2) + NH // 2 - 1
                aog.append(v_aol[last_it])
            return aog


        def emit_wo(blk, h, aog):
            srcs = []
            for oc in range(NDC):
                srcs.append(wo_[oc][:, 0:512])
                srcs.append(wo_[oc][:, 512:1024])
            fs = FStream(blk, srcs)
            add_mk = {}
            wadd = {}
            for oc in range(NDC):
                for g in range(NTG):
                    u = oc * NTG + g
                    bank = pb[u % 4]

                    @blk.tensor
                    def _(t, oc=oc, g=g, u=u, bank=bank):
                        fs.twait(t, 2 * oc)
                        if u >= 4:
                            t.wait_ge(sh("sv"), add_mk[u - 4])
                        mm3h(t, bank[:, 0:TG], fs.buf(2 * oc),
                             lambda kc: r_hi(kc)[:, g * TG:(g + 1) * TG],
                             lambda kc: r_lo(kc)[:, g * TG:(g + 1) * TG],
                             0, 4, True, False)
                        fs.twait(t, 2 * oc + 1)
                        mm = mm3h(t, bank[:, 0:TG], fs.buf(2 * oc + 1),
                                  lambda kc: r_hi(kc)[:, g * TG:(g + 1) * TG],
                                  lambda kc: r_lo(kc)[:, g * TG:(g + 1) * TG],
                                  4, 4, False, True)
                        inc(mm, "st")
                    if g == NTG - 1:
                        fs.done(2 * oc)
                        fs.done(2 * oc + 1)
                    t_mk = cnt("st")

                    @blk.vector
                    def _(v, oc=oc, g=g, bank=bank, tm=t_mk):
                        v.wait_ge(sh("st"), tm)
                        gb = g * TG
                        inc(v.scalar_tensor_tensor(
                            X[oc][:, gb:gb + TG], bank[:, 0:TG], 1.0,
                            X[oc][:, gb:gb + TG], op0=ALU.mult,
                            op1=ALU.add), "sv")
                    add_mk[u] = cnt("sv")
                    wadd[(oc, g)] = add_mk[u]
            return wadd, cnt("st")

        def emit_glu_g(blk, h, g):
            gb = g * TG
            srcs = []
            for f in range(NFC):
                for wi in range(2):
                    srcs.append(wgu[2 * f + wi][:, 0:512])
                    srcs.append(wgu[2 * f + wi][:, 512:1024])
            fs = FStream(blk, srcs)
            hg_mk = {}
            sil_mk_d = {}
            hf_mk_d = {}
            for f in range(NFC):
                sbA = (tmp_b, rstd_b) if f % 2 == 0 else (stp[0], stp[1])
                for wi in range(2):
                    ju = (2 * f + wi) * 2
                    bank = pb[wi + 2 * (f % 2)]

                    @blk.tensor
                    def _(t, f=f, wi=wi, ju=ju, bank=bank):
                        fs.twait(t, ju)
                        if f >= 2 and wi == 0:
                            t.wait_ge(sh("ss"), sil_mk_d[f - 2])
                        if f >= 2 and wi == 1:
                            t.wait_ge(sh("sv"), hf_mk_d[f - 2])
                        mm3h(t, bank[:, 0:TG], fs.buf(ju),
                             lambda kc: r_hi(kc)[:, gb:gb + TG],
                             lambda kc: r_lo(kc)[:, gb:gb + TG],
                             0, 4, True, False)
                        fs.twait(t, ju + 1)
                        mm = mm3h(t, bank[:, 0:TG], fs.buf(ju + 1),
                                  lambda kc: r_hi(kc)[:, gb:gb + TG],
                                  lambda kc: r_lo(kc)[:, gb:gb + TG],
                                  4, 4, False, True)
                        inc(mm, "st")
                    fs.done(ju)
                    fs.done(ju + 1)
                t_mk = cnt("st")
                pg_ = pb[0 + 2 * (f % 2)]
                pu_ = pb[1 + 2 * (f % 2)]

                @blk.scalar
                def _(s, f=f, pg_=pg_, tm=t_mk, sbA=sbA,
                      sf=hf_mk_d.get(f - 2, 0)):
                    s.wait_ge(sh("st"), tm)
                    wt(s, "sv", sf)
                    inc(s.activation(sbA[0][:, 0:TG], pg_[:, 0:TG],
                                     AF.Silu), "ss")
                sil_mk_d[f] = cnt("ss")

                @blk.vector
                def _(v, f=f, pu_=pu_, sm=sil_mk_d[f], sbA=sbA,
                      hf=hg_mk.get(f - 2, 0)):
                    v.wait_ge(sh("ss"), sm)
                    wt(v, "sv", hf)
                    inc(v.scalar_tensor_tensor(
                        sbA[1][:, 0:TG], pu_[:, 0:TG], 1.0,
                        sbA[0][:, 0:TG], op0=ALU.mult, op1=ALU.mult),
                        "sv")
                hf_mk_d[f] = cnt("sv")

                @blk.vector
                def _(v, f=f, hm=hf_mk_d[f], sbA=sbA):
                    v.wait_ge(sh("sv"), hm)
                    inc(v.tensor_copy(ha_hi(f)[:, 0:TG],
                                      sbA[1][:, 0:TG]), "sv")
                hi_mk = cnt("sv")

                @blk.vector
                def _(v, f=f, im=hi_mk, sbA=sbA):
                    v.wait_ge(sh("sv"), im)
                    inc(v.scalar_tensor_tensor(
                        ha_lo(f)[:, 0:TG], sbA[1][:, 0:TG], 1.0,
                        ha_hi(f)[:, 0:TG], op0=ALU.mult,
                        op1=ALU.subtract), "sv")
                hg_mk[f] = cnt("sv")

            srcs2 = []
            for oc in range(NDC):
                for q in range(8):
                    srcs2.append(wout[oc][:, q * 512:(q + 1) * 512])
            fs2 = FStream(blk, srcs2)
            add_mk = {}
            for oc in range(NDC):
                bank = pb[4 + oc % 2]
                for q in range(8):
                    ju = oc * 8 + q

                    @blk.tensor
                    def _(t, oc=oc, q=q, ju=ju, bank=bank):
                        fs2.twait(t, ju)
                        if q == 0:
                            t.wait_ge(sh("sv"), hg_mk[NFC - 1])
                            if oc >= 2:
                                t.wait_ge(sh("sv"), add_mk[oc - 2])
                        mm = mm3h(t, bank[:, 0:TG], fs2.buf(ju),
                                  lambda kc, q=q: ha_hi(q * 4 + kc)[:, 0:TG],
                                  lambda kc, q=q: ha_lo(q * 4 + kc)[:, 0:TG],
                                  0, 4, q == 0, q == 7)
                        inc(mm, "st")
                    fs2.done(ju)
                t_mk = cnt("st")

                @blk.vector
                def _(v, oc=oc, bank=bank, tm=t_mk):
                    v.wait_ge(sh("st"), tm)
                    inc(v.scalar_tensor_tensor(
                        X[oc][:, gb:gb + TG], bank[:, 0:TG], 1.0,
                        X[oc][:, gb:gb + TG], op0=ALU.mult,
                        op1=ALU.add), "sv")
                add_mk[oc] = cnt("sv")

            return add_mk, cnt("st"), hg_mk[NFC - 1]

        def emit_ln1add(blk, h, gadd_last):
            add_mk = {}
            e_sv = cnt("sv")
            hw_w = 512 if H >= 1024 else H // 2
            for c in range(NDC):
                for hw in range(2):
                    j = c * 2 + hw
                    par = j % 2
                    lastg = ((hw + 1) * hw_w - 1) // TG == NTG - 1

                    @blk.sync
                    def _(sync, c=c, hw=hw, par=par, j=j, hw_w=hw_w,
                          lastg=lastg):
                        if j < 2:
                            wt(sync, "sv", e_sv)
                        if j >= 2:
                            wt(sync, "sv", add_mk[(j - 2) // 2, (j - 2) % 2])
                        if lastg:
                            wt(sync, "sv", gadd_last[c])
                        inc(sync.dma_start(
                            out=wstg[par][:, 0:hw_w],
                            in_=xt[c * 128:(c + 1) * 128,
                                   h * H + hw * hw_w:
                                   h * H + (hw + 1) * hw_w]),
                            "dw%d" % par, 16)
                    dmk = cnt("dw%d" % par)

                    @blk.vector
                    def _(v, c=c, hw=hw, par=par, dmk=dmk, hw_w=hw_w,
                          lastg=lastg):
                        v.wait_ge(sh("dw%d" % par), dmk)
                        if lastg:
                            wt(v, "sv", gadd_last[c])
                        inc(v.tensor_tensor(
                            X[c][:, hw * hw_w:(hw + 1) * hw_w],
                            X[c][:, hw * hw_w:(hw + 1) * hw_w],
                            wstg[par][:, 0:hw_w], op=ALU.add), "sv")
                    add_mk[(c, hw)] = cnt("sv")
            return add_mk

        def emit_ffnA(blk, h, g):
            gb = g * TG
            for c in range(NDC):
                @blk.vector
                def _(v, c=c):
                    inc(v.tensor_copy(x4g_hi(c)[:, 0:TG],
                                      X[c][:, gb:gb + TG]), "sv")
                hm = cnt("sv")

                @blk.vector
                def _(v, c=c, hm=hm):
                    v.wait_ge(sh("sv"), hm)
                    inc(v.scalar_tensor_tensor(
                        x4g_lo(c)[:, 0:TG], X[c][:, gb:gb + TG], 1.0,
                        x4g_hi(c)[:, 0:TG], op0=ALU.mult,
                        op1=ALU.subtract), "sv")
            x4_v = cnt("sv")
            dg_mk = {}
            used_mk = {}
            ha_mk = {}
            nsl = 2 * NFC

            def tput(j):
                slot = j % 4
                par, off = slot // 2, (slot % 2) * D
                free = [("st", used_mk[j - 4])] if j >= 4 else []

                @blk.gpsimd
                def _(gp, j=j, par=par, off=off, free=free):
                    for s_, v_ in free:
                        wt(gp, s_, v_)
                    inc(gp.dma_start(out=tsl[par][:, off:off + D],
                                     in_=w12[j]), "dg%d" % slot, 16)
                dg_mk[j] = (par, off, slot, cnt("dg%d" % slot))

            for j in range(min(4, nsl)):
                tput(j)
            sil_mk_d = {}
            hf_mk_d = {}
            for f in range(NFC):
                sbA = (tmp_b, rstd_b) if f % 2 == 0 else (stp[0], stp[1])
                for wi in range(2):
                    j = 2 * f + wi
                    par, off, slot, dmk = dg_mk[j]
                    bank = pb[wi + 2 * (f % 2)]

                    @blk.tensor
                    def _(t, f=f, wi=wi, par=par, off=off, slot=slot,
                          dmk=dmk, bank=bank):
                        t.wait_ge(sh("dg%d" % slot), dmk)
                        if f == 0 and wi == 0:
                            t.wait_ge(sh("sv"), x4_v)
                        if f >= 2 and wi == 0:
                            t.wait_ge(sh("ss"), sil_mk_d[f - 2])
                        if f >= 2 and wi == 1:
                            t.wait_ge(sh("sv"), hf_mk_d[f - 2])
                        mm = None
                        for kc in range(NDC):
                            lh = tsl[par][:, off + kc * 128:
                                          off + (kc + 1) * 128]
                            t.matmul(bank[:, 0:TG], lh,
                                     x4g_hi(kc)[:, 0:TG],
                                     start=(kc == 0), stop=False)
                            mm = t.matmul(bank[:, 0:TG], lh,
                                          x4g_lo(kc)[:, 0:TG],
                                          start=False,
                                          stop=(kc == NDC - 1))
                        inc(mm, "st")
                    used_mk[j] = cnt("st")
                    if j + 4 < nsl:
                        tput(j + 4)
                t_mk = cnt("st")
                pg_ = pb[0 + 2 * (f % 2)]
                pu_ = pb[1 + 2 * (f % 2)]

                @blk.scalar
                def _(s, f=f, pg_=pg_, tm=t_mk, sbA=sbA,
                      sf=hf_mk_d.get(f - 2, 0)):
                    s.wait_ge(sh("st"), tm)
                    wt(s, "sv", sf)
                    inc(s.activation(sbA[0][:, 0:TG], pg_[:, 0:TG],
                                     AF.Silu, scale=c_s1[:, f:f + 1]),
                        "ss")
                sil_mk_d[f] = cnt("ss")

                @blk.vector
                def _(v, f=f, pu_=pu_, sm=sil_mk_d[f], sbA=sbA,
                      hf=ha_mk.get(f - 2, 0)):
                    v.wait_ge(sh("ss"), sm)
                    wt(v, "sv", hf)
                    inc(v.scalar_tensor_tensor(
                        sbA[1][:, 0:TG], pu_[:, 0:TG], c_s2[:, f:f + 1],
                        sbA[0][:, 0:TG], op0=ALU.mult, op1=ALU.mult),
                        "sv")
                hf_mk_d[f] = cnt("sv")

                @blk.vector
                def _(v, f=f, hm=hf_mk_d[f], sbA=sbA):
                    v.wait_ge(sh("sv"), hm)
                    inc(v.tensor_copy(ha_hi(f)[:, 0:TG],
                                      sbA[1][:, 0:TG]), "sv")
                hi_mk = cnt("sv")

                @blk.vector
                def _(v, f=f, im=hi_mk, sbA=sbA):
                    v.wait_ge(sh("sv"), im)
                    inc(v.scalar_tensor_tensor(
                        ha_lo(f)[:, 0:TG], sbA[1][:, 0:TG], 1.0,
                        ha_hi(f)[:, 0:TG], op0=ALU.mult,
                        op1=ALU.subtract), "sv")
                ha_mk[f] = cnt("sv")


        def emit_hstage(blk, wsrc, src_hi, src_lo, dst_hi, dst_lo, sc):
            dg_mk = {}
            used_mk = {}
            out_mk = {}

            def tput(u):
                par = u % 2
                free = [("st", used_mk[u - 2])] if u >= 2 else []

                @blk.gpsimd
                def _(gp, u=u, par=par, free=free):
                    for s_, v_ in free:
                        wt(gp, s_, v_)
                    f, hw = u // 2, u % 2
                    inc(gp.dma_start(
                        out=tsl[par][:],
                        in_=wsrc[f][:, hw * 2048:(hw + 1) * 2048]),
                        "dg%d" % par, 16)
                dg_mk[u] = cnt("dg%d" % (u % 2))

            tput(0)
            tput(1)
            for f in range(NFC):
                bank = pb[f % 3]
                for hw in range(2):
                    u = 2 * f + hw

                    @blk.tensor
                    def _(t, f=f, hw=hw, u=u, bank=bank):
                        t.wait_ge(sh("dg%d" % (u % 2)), dg_mk[u])
                        if hw == 0 and f >= 3:
                            t.wait_ge(sh("sv"), out_mk[f - 3])
                        mm = None
                        for kc in range(16):
                            kk = hw * 16 + kc
                            lh = tsl[u % 2][:, kc * 128:(kc + 1) * 128]
                            t.matmul(bank[:, 0:TG], lh,
                                     src_hi(kk)[:, 0:TG],
                                     start=(hw == 0 and kc == 0),
                                     stop=False)
                            mm = t.matmul(
                                bank[:, 0:TG], lh,
                                src_lo(kk)[:, 0:TG], start=False,
                                stop=(hw == 1 and kc == 15))
                        inc(mm, "st")
                    used_mk[u] = cnt("st")
                    if u + 2 < 2 * NFC:
                        tput(u + 2)
                t_mk = cnt("st")

                @blk.scalar
                def _(s, f=f, bank=bank, tm=t_mk):
                    s.wait_ge(sh("st"), tm)
                    if f >= 2:
                        wt(s, "sv", out_mk[f - 2])
                    inc(s.activation(dst_hi(f)[:, 0:TG],
                                     bank[:, 0:TG], AF.Gelu,
                                     scale=sc[:, f:f + 1]), "ss")
                    inc(s.activation(stp[f % 2][:, 0:TG],
                                     bank[:, 0:TG], AF.Gelu,
                                     scale=sc[:, f:f + 1]), "ss")
                s_mk = cnt("ss")

                @blk.vector
                def _(v, f=f, sm=s_mk):
                    v.wait_ge(sh("ss"), sm)
                    inc(v.scalar_tensor_tensor(
                        dst_lo(f)[:, 0:TG], stp[f % 2][:, 0:TG],
                        1.0, dst_hi(f)[:, 0:TG], op0=ALU.mult,
                        op1=ALU.subtract), "sv")
                out_mk[f] = cnt("sv")

        def emit_ffnD(blk, h, g):
            gb = g * TG
            dg_mk = {}
            used_mk = {}
            add_mk = {}

            def tput(u):
                par = u % 2
                free = [("st", used_mk[u - 2])] if u >= 2 else []

                @blk.gpsimd
                def _(gp, u=u, par=par, free=free):
                    for s_, v_ in free:
                        wt(gp, s_, v_)
                    oc, hw = u // 2, u % 2
                    inc(gp.dma_start(
                        out=tsl[par][:],
                        in_=w3_[oc][:, hw * 2048:(hw + 1) * 2048]),
                        "dg%d" % par, 16)
                dg_mk[u] = cnt("dg%d" % (u % 2))

            tput(0)
            tput(1)
            for oc in range(NDC):
                bank = pb[4 + oc % 2]
                for hw in range(2):
                    u = 2 * oc + hw

                    @blk.tensor
                    def _(t, oc=oc, hw=hw, u=u, bank=bank):
                        t.wait_ge(sh("dg%d" % (u % 2)), dg_mk[u])
                        if hw == 0 and oc >= 2:
                            t.wait_ge(sh("sv"), add_mk[oc - 2])
                        mm = None
                        for kc in range(16):
                            kk = hw * 16 + kc
                            lh = tsl[u % 2][:, kc * 128:(kc + 1) * 128]
                            t.matmul(bank[:, 0:TG], lh,
                                     ha_hi(kk)[:, 0:TG],
                                     start=(hw == 0 and kc == 0),
                                     stop=False)
                            mm = t.matmul(bank[:, 0:TG], lh,
                                          ha_lo(kk)[:, 0:TG],
                                          start=False,
                                          stop=(hw == 1 and kc == 15))
                        inc(mm, "st")
                    used_mk[u] = cnt("st")
                    if u + 2 < 2 * NDC:
                        tput(u + 2)
                t_mk = cnt("st")

                @blk.vector
                def _(v, oc=oc, bank=bank, tm=t_mk):
                    v.wait_ge(sh("st"), tm)
                    inc(v.scalar_tensor_tensor(
                        X[oc][:, gb:gb + TG], bank[:, 0:TG],
                        c_s3[:, oc:oc + 1], X[oc][:, gb:gb + TG],
                        op0=ALU.mult, op1=ALU.add), "sv")
                add_mk[oc] = cnt("sv")

            return add_mk, cnt("st")

        def dump_dbg():
            with nc.Block() as blk:
                @blk.sync
                def _(sync):
                    for c in range(NDC):
                        inc(sync.dma_start(
                            out=dbg[c * 128:(c + 1) * 128, :], in_=X[c][:]),
                            "do", 16)
                    wt(sync, "do")

        for h in range(2):
            with nc.named_scope(f"loadx{h}"), nc.Block() as blk:
                emit_loadx(blk, h)
                rdone = norm_rms(blk, c_gmix)

            with nc.named_scope(f"mixmm{h}"), nc.Block() as blk:
                xadd, mml = emit_mixmm(blk, h, rdone)
                rdone = norm_rms(
                    blk, c_gattn,
                    xmk=lambda c, g, xadd=xadd: [("sv", xadd[(c, g)])],
                    rfree=("st", mml))
            if stop_after == "mix" and h == 0:
                dump_dbg()
                return nc

            with nc.named_scope(f"attn{h}"), nc.Block() as blk:
                emit_atv(blk, h)
                emit_qk(blk, h, 1)
                emit_qk(blk, h, 0, qfree=("st", cnt("st")))
                aog = emit_atloop(blk, h)
                ao_sv = cnt("sv")
                wadd, wol = emit_wo(blk, h, aog)
                rdone = norm_rms(
                    blk, c_gffn,
                    xmk=lambda c, g, wadd=wadd: [("sv", wadd[(c, g)])],
                    rfree=("st", wol), entry_sv=ao_sv)
            if stop_after == "attn" and h == 0:
                dump_dbg()
                return nc

            gadd = None
            for g in range(NTG):
                last = (g == NTG - 1)
                with nc.named_scope(f"glumm{h}_{g}"), nc.Block() as blk:
                    gadd, glul, hg_last = emit_glu_g(blk, h, g)
                    if last:
                        lnadd = emit_ln1add(blk, h, gadd)
                        hw_w = 512 if H >= 1024 else H // 2

                        def ln1_xmk(c, g2, lnadd=lnadd, hw_w=hw_w):
                            ws = []
                            for hw in range(2):
                                t0, t1 = hw * hw_w, (hw + 1) * hw_w
                                if t0 < (g2 + 1) * TG and t1 > g2 * TG:
                                    ws.append(("sv", lnadd[(c, hw)]))
                            return ws
                        norm_ln(blk, xmk=ln1_xmk, entry_sv=hg_last)
            if stop_after == "glu" and h == 0:
                dump_dbg()
                return nc

            dadd = None
            for g in range(NTG):
                last = (g == NTG - 1)
                with nc.named_scope(f"ffnA{h}_{g}"), nc.Block() as blk:
                    emit_ffnA(blk, h, g)
                with nc.named_scope(f"ffnB{h}_{g}"), nc.Block() as blk:
                    emit_hstage(blk, wh1, ha_hi, ha_lo, hb_hi, hb_lo, c_sh1)
                with nc.named_scope(f"ffnC{h}_{g}"), nc.Block() as blk:
                    emit_hstage(blk, wh2, hb_hi, hb_lo, ha_hi, ha_lo, c_sh2)
                with nc.named_scope(f"ffnD{h}_{g}"), nc.Block() as blk:
                    blk_sv = cnt("sv")
                    dadd, dl = emit_ffnD(blk, h, g)
                    if last:
                        def ln2_xmk(c, g2, dadd=dadd, g=g):
                            if g2 == g:
                                return [("sv", dadd[c])]
                            return []
                        norm_ln(blk, xmk=ln2_xmk, entry_sv=blk_sv)

                        @blk.sync
                        def _(sync, h=h):
                            wt(sync, "sv")
                            for c in range(NDC):
                                inc(sync.dma_start(
                                    out=out[c * 128:(c + 1) * 128,
                                            h * H:(h + 1) * H],
                                    in_=X[c][:]), "do", 16)
            if stop_after == "ffn" and h == 0:
                dump_dbg()
                return nc

        with nc.Block() as blk:
            @blk.sync
            def _(sync):
                wt(sync, "do")

    return nc



def _lhsT_slabs(w):
    O, K = w.shape
    a = np.ascontiguousarray(w.T).reshape(K // 128, 128, O // 128, 128)
    a = a.transpose(2, 1, 0, 3)
    return np.ascontiguousarray(a.reshape(O // 128, 128, K))


def _interleave(a, b):
    o = np.empty((a.shape[0] * 2,) + a.shape[1:], dtype=a.dtype)
    o[0::2] = a
    o[1::2] = b
    return np.ascontiguousarray(o)


def _pp(vec):
    return np.ascontiguousarray(np.asarray(vec).reshape(-1, 128).T)


def _make_masks():
    j = np.arange(128)[:, None]
    k = np.arange(256)[None, :]
    valid = (k <= 128 + j) & (k >= j + 1)
    mg = np.where(valid, 0.0, NEG).astype(np.float32)
    m0 = np.where(valid & (k >= 128), 0.0, NEG).astype(np.float32)
    m = np.concatenate([m0, mg], axis=1)
    return np.ascontiguousarray(m.astype(ml_dtypes.bfloat16))


def stage_weights(inputs):
    s = {}
    g3 = _lhsT_slabs(np.asarray(inputs["L0_attn_Wg"]))
    v3 = _lhsT_slabs(np.asarray(inputs["L0_attn_Wv"]))
    d3 = _lhsT_slabs(np.asarray(inputs["L0_attn_Wd"]))
    gvd = np.empty((3 * NDC, 128, D), dtype=np.float32)
    gvd[0::3], gvd[1::3], gvd[2::3] = g3, v3, d3
    s["wgvd"] = np.ascontiguousarray(gvd)
    wqkv = np.asarray(inputs["L0_attn_Wqkv"])
    s["wqk"] = _lhsT_slabs(wqkv[0:2 * D])
    wv = wqkv[2 * D:3 * D]
    s["wvr"] = np.ascontiguousarray(
        np.ascontiguousarray(wv.T).reshape(NDC, 128, D))
    s["wo"] = _lhsT_slabs(np.asarray(inputs["L0_attn_Wo"]))
    s["wgu"] = _interleave(_lhsT_slabs(np.asarray(inputs["L0_attn_Wgate"])),
                           _lhsT_slabs(np.asarray(inputs["L0_attn_Wup"])))
    s["wout"] = _lhsT_slabs(np.asarray(inputs["L0_attn_Wout"]))
    s["w12"] = _interleave(_lhsT_slabs(np.asarray(inputs["L0_ffn_w1_w"])),
                           _lhsT_slabs(np.asarray(inputs["L0_ffn_w2_w"])))
    s["wh1"] = _lhsT_slabs(np.asarray(inputs["L0_ffn_h1_w"]))
    s["wh2"] = _lhsT_slabs(np.asarray(inputs["L0_ffn_h2_w"]))
    s["w3"] = _lhsT_slabs(np.asarray(inputs["L0_ffn_w3_w"]))
    vecs = np.concatenate([
        _pp(inputs["L0_attn_w_rms_mix"]), _pp(inputs["L0_attn_w_rms_attn"]),
        _pp(inputs["L0_attn_w_rms_ffn"]), _pp(inputs["L0_attn_bg"]),
        _pp(inputs["L0_attn_bv"]), _pp(inputs["L0_attn_bd"]),
        _pp(inputs["L0_ffn_w1_s"]), _pp(inputs["L0_ffn_w2_s"]),
        _pp(inputs["L0_ffn_h1_s"]), _pp(inputs["L0_ffn_h2_s"]),
        _pp(inputs["L0_ffn_w3_s"]),
    ], axis=1)
    s["vecs"] = np.ascontiguousarray(vecs.astype(np.float32))
    s["idenf"] = np.eye(128, dtype=np.float32)
    s["idenb"] = np.eye(128, dtype=np.float32).astype(ml_dtypes.bfloat16)
    s["onesf"] = np.ones((128, 128), dtype=np.float32)
    s["masks"] = _make_masks()
    return s


def kernel(**inputs):
    T = 2048
    x = np.asarray(inputs["L0_attn_x"], dtype=np.float32)
    B = x.shape[0]
    shared = stage_weights(inputs)
    nc = build_nc(T)
    in_maps = []
    for b in range(B):
        m = dict(shared)
        m["xt"] = np.ascontiguousarray(x[b].T)
        in_maps.append(m)
    res = run_bass_kernel_spmd(nc, in_maps, core_ids=list(range(B)))
    outs = [np.ascontiguousarray(np.asarray(r["out"]).T) for r in res.results]
    return np.stack(outs).astype(np.float32)


# version 14
# speedup vs baseline: 1.0456x; 1.0456x over previous
import contextlib
import numpy as np
import ml_dtypes

import concourse.bass as bass
import concourse.mybir as mybir
from concourse.bass_utils import run_bass_kernel_spmd

F32 = mybir.dt.float32
BF16 = mybir.dt.bfloat16
AF = mybir.ActivationFunctionType
ALU = mybir.AluOpType
AX = mybir.AxisListType

D = 1024
DFF = 4096
NH = 16
NDC = D // 128
NFC = DFF // 128
EPS_RMS = 1e-6
EPS_LN = 1e-5
NEG = -1e30


def build_nc(T=2048, stop_after=None):
    H = T // 2
    TG = min(512, H)
    NTG = H // TG
    NTC = H // 128

    nc = bass.Bass("TRN2", target_bir_lowering=False)

    def din(name, shape, dt=F32):
        return nc.dram_tensor(name, shape, dt, kind="ExternalInput").ap()

    xt = din("xt", [D, T])
    wgvd = din("wgvd", [3 * NDC, 128, D])
    wqk = din("wqk", [16, 128, D])
    wvr = din("wvr", [NDC, 128, D])
    wo_ = din("wo", [NDC, 128, D])
    wgu = din("wgu", [2 * NFC, 128, D])
    wout = din("wout", [NDC, 128, DFF])
    w12 = din("w12", [2 * NFC, 128, D])
    wh1 = din("wh1", [NFC, 128, DFF])
    wh2 = din("wh2", [NFC, 128, DFF])
    w3_ = din("w3", [NDC, 128, DFF])
    vecs = din("vecs", [128, 6 * NDC + 4 * NFC + NDC])
    idenf = din("idenf", [128, 128])
    idenb = din("idenb", [128, 128], BF16)
    onesf = din("onesf", [128, 128])
    masks = din("masks", [128, 512], BF16)

    out = nc.dram_tensor("out", [D, T], F32, kind="ExternalOutput").ap()
    dbg = None
    dbg2 = None
    if stop_after is not None:
        dbg = nc.dram_tensor("dbg", [D, H], F32, kind="ExternalOutput").ap()
        dbg2 = nc.dram_tensor("dbg2", [H, D], F32, kind="ExternalOutput").ap()

    RB = 0
    QB = 16 * H
    KB = QB + 16 * H
    VB = KB + 16 * H
    VEND = VB + 2 * NTC * 1024
    if 16 * H >= 16 * 1024:
        VR = QB
        VREND = VEND
    else:
        VR = VEND
        VREND = VR + 16 * 1024
    HA = QB
    if 16 * H >= 32 * TG:
        HBH = RB
        HBL = QB + 64 * TG
    else:
        HBH = HA + 64 * TG
        HBL = HBH + 32 * TG
    PTB = max(VEND, VREND, HBL + 32 * TG, QB + 64 * TG)
    KPB = PTB + 2048
    VPB = KPB + 2048
    ARN = VPB + 2048
    assert ARN <= 74000, f"arena too big: {ARN}"

    ctx = contextlib.ExitStack()
    E = ctx.enter_context
    with ctx:
        X = [E(nc.sbuf_tensor(f"X{c}", [128, H], F32)) for c in range(NDC)]
        ARE = E(nc.sbuf_tensor("ARE", [128, ARN], BF16))
        stp = [E(nc.sbuf_tensor(f"stp{i}", [128, 512], F32)) for i in range(2)]
        mean_b = E(nc.sbuf_tensor("mean_b", [128, 512], F32))
        rstd_b = E(nc.sbuf_tensor("rstd_b", [128, 512], F32))
        tmp_b = E(nc.sbuf_tensor("tmp_b", [128, 512], F32))
        wstg = [E(nc.sbuf_tensor(f"wstg{i}", [128, 512], F32)) for i in range(2)]
        sp = [E(nc.sbuf_tensor(f"sp{i}", [128, 256], F32)) for i in range(2)]
        scn_c = E(nc.sbuf_tensor("scn_c", [128, NDC], F32))
        nmb = E(nc.sbuf_tensor("nmb", [128, 4], F32))
        smb = E(nc.sbuf_tensor("smb", [128, 4], F32))
        rrb = E(nc.sbuf_tensor("rrb", [128, 4], F32))
        VEC = E(nc.sbuf_tensor("VEC", [128, 6 * NDC + 4 * NFC + NDC], F32))
        c_idf = E(nc.sbuf_tensor("c_idf", [128, 128], F32))
        c_idb = E(nc.sbuf_tensor("c_idb", [128, 128], BF16))
        c_ones = E(nc.sbuf_tensor("c_ones", [128, 128], F32))
        c_mask = E(nc.sbuf_tensor("c_mask", [128, 512], BF16))
        wsl_hi = [E(nc.sbuf_tensor(f"wslh{i}", [128, 512], BF16)) for i in range(4)]
        wsl_lo = [E(nc.sbuf_tensor(f"wsll{i}", [128, 512], BF16)) for i in range(4)]
        tsl = [E(nc.sbuf_tensor(f"tsl{i}", [128, 2048], BF16)) for i in range(2)]
        pb = [E(nc.psum_tensor(f"pb{i}", [128, 512], F32)) for i in range(8)]
        nc._sbuf_left = nc.sbuf_bytes_remaining

        def vv(i0, n):
            return VEC[:, i0:i0 + n]
        c_gmix = vv(0, NDC)
        c_gattn = vv(NDC, NDC)
        c_gffn = vv(2 * NDC, NDC)
        c_bg = vv(3 * NDC, NDC)
        c_bv = vv(4 * NDC, NDC)
        c_bd = vv(5 * NDC, NDC)
        c_s1 = vv(6 * NDC, NFC)
        c_s2 = vv(6 * NDC + NFC, NFC)
        c_sh1 = vv(6 * NDC + 2 * NFC, NFC)
        c_sh2 = vv(6 * NDC + 3 * NFC, NFC)
        c_s3 = vv(6 * NDC + 4 * NFC, NDC)

        SEM = {}
        for s in ("dx", "dw0", "dw1", "dg0", "dg1", "dg2", "dg3", "do",
                  "st", "ss", "sv"):
            SEM[s] = [E(nc.semaphore(s)), 0]

        def sh(s):
            return SEM[s][0]

        def cnt(s):
            return SEM[s][1]

        def inc(instr, s, by=1):
            SEM[s][1] += by
            instr.then_inc(SEM[s][0], by)
            return SEM[s][1]

        def wt(eng, s, val=None):
            v = SEM[s][1] if val is None else val
            if v > 0:
                eng.wait_ge(SEM[s][0], v)

        def av(off, n):
            return ARE[:, off:off + n]

        def r_hi(c):
            return av(RB + c * H, H)

        def r_lo(c):
            return av(RB + (8 + c) * H, H)

        def q_hi(c):
            return av(QB + c * H, H)

        def q_lo(c):
            return av(QB + (8 + c) * H, H)

        def vr_hi(kc):
            return av(VR + kc * 1024, 1024)

        def vr_lo(kc):
            return av(VR + (8 + kc) * 1024, 1024)

        def k_hi(c):
            return av(KB + c * H, H)

        def k_lo(c):
            return av(KB + (8 + c) * H, H)

        def kp_hi(c):
            return av(KPB + c * 128, 128)

        def kp_lo(c):
            return av(KPB + (8 + c) * 128, 128)

        def v_hi(t):
            return av(VB + (t - 1) * 1024, 1024)

        def v_lo(t):
            return av(VB + (NTC + t - 1) * 1024, 1024)

        def vp_hi():
            return av(VPB, 1024)

        def vp_lo():
            return av(VPB + 1024, 1024)

        def x4g_hi(c):
            return av(c * TG, TG)

        def x4g_lo(c):
            return av((8 + c) * TG, TG)

        def ha_hi(f):
            return av(HA + f * TG, TG)

        def ha_lo(f):
            return av(HA + (32 + f) * TG, TG)

        def hb_hi(f):
            return av(HBH + f * TG, TG)

        def hb_lo(f):
            return av(HBL + f * TG, TG)

        def pt_hi(slot):
            return av(PTB + slot * 512, 256)

        def pt_lo(slot):
            return av(PTB + slot * 512 + 256, 256)

        with nc.Block() as blk:
            @blk.sync
            def _(sync):
                for dst, src in [(VEC, vecs), (c_idf, idenf), (c_idb, idenb),
                                 (c_ones, onesf), (c_mask, masks)]:
                    inc(sync.dma_start(out=dst[:], in_=src), "dx", 16)

            @blk.vector
            def _(v):
                for c in range(NDC):
                    v.memset(kp_hi(c), 0.0)
                    v.memset(kp_lo(c), 0.0)
                v.memset(vp_hi(), 0.0)
                v.memset(vp_lo(), 0.0)
                v.memset(scn_c[:], 0.0)
        CONST_DX = cnt("dx")

        class FStream:

            def __init__(self, blk, srcs):
                self.blk = blk
                self.srcs = srcs
                self.split = {}
                self.used = {}
                self.free0 = [("ss", cnt("ss")), ("sv", cnt("sv")),
                              ("st", cnt("st"))]
                self.j = 0
                for _ in range(4):
                    self.put()

            def put(self):
                j = self.j
                if j >= len(self.srcs):
                    return
                self.j += 1
                src = self.srcs[j]
                blk = self.blk
                kl = src.shape[-1]
                par, q = j % 2, j % 4
                if j >= 2:
                    fdma = [("sv", self.split[j - 2][1])]
                else:
                    fdma = self.free0
                fwsl = [("st", self.used[j - 4])] if j >= 4 else self.free0

                @blk.sync
                def _(sync):
                    for s_, v_ in fdma:
                        wt(sync, s_, v_)
                    inc(sync.dma_start(out=wstg[par][:, 0:kl], in_=src),
                        "dw%d" % par, 16)
                dmk = cnt("dw%d" % par)

                @blk.vector
                def _(v):
                    v.wait_ge(sh("dw%d" % par), dmk)
                    for s_, v_ in fwsl:
                        wt(v, s_, v_)
                    inc(v.tensor_copy(wsl_hi[q][:, 0:kl], wstg[par][:, 0:kl]),
                        "sv")
                hi_mk = cnt("sv")

                @blk.vector
                def _(v):
                    v.wait_ge(sh("sv"), hi_mk)
                    inc(v.scalar_tensor_tensor(
                        wsl_lo[q][:, 0:kl], wstg[par][:, 0:kl], 1.0,
                        wsl_hi[q][:, 0:kl], op0=ALU.mult, op1=ALU.subtract),
                        "sv")
                self.split[j] = (hi_mk, cnt("sv"), q)

            def twait(self, t, j):
                t.wait_ge(sh("sv"), self.split[j][1])

            def buf(self, j):
                return self.split[j][2]

            def done(self, j):
                self.used[j] = cnt("st")
                self.put()

        def mm3h(t, psum, q, rhs_hi, rhs_lo, kbase, nkc, first, last):
            mm = None
            for kc in range(nkc):
                hi = wsl_hi[q][:, kc * 128:(kc + 1) * 128]
                lo = wsl_lo[q][:, kc * 128:(kc + 1) * 128]
                t.matmul(psum, hi, rhs_hi(kbase + kc),
                         start=(first and kc == 0), stop=False)
                t.matmul(psum, hi, rhs_lo(kbase + kc), start=False, stop=False)
                mm = t.matmul(psum, lo, rhs_hi(kbase + kc), start=False,
                              stop=(last and kc == nkc - 1))
            return mm

        def norm_rms(blk, gamma, xmk=None, rfree=None, entry_sv=None):
            rdone = []
            dx_all = cnt("dx")
            e_sv = cnt("sv") if entry_sv is None else entry_sv
            e_ss = cnt("ss")
            sq_mk = [0, 0]
            mm_mk = [0, 0]
            pb6_free = 0
            tmp_free = 0
            rstd_free = 0
            hi_mk = [0, 0]
            lo_mk = [0, 0]
            for g in range(NTG):
                gb = g * TG
                for c in range(NDC):
                    @blk.scalar
                    def _(s, c=c, g=g):
                        if g == 0 and c == 0:
                            s.wait_ge(sh("dx"), dx_all)
                            wt(s, "sv", e_sv)
                        if xmk is not None:
                            for s_, v_ in xmk(c, g):
                                wt(s, s_, v_)
                        wt(s, "st", mm_mk[c % 2])
                        wt(s, "sv", lo_mk[c % 2])
                        inc(s.activation(stp[c % 2][:, 0:TG],
                                         X[c][:, gb:gb + TG], AF.Square), "ss")
                    sq_mk[c % 2] = cnt("ss")

                    @blk.tensor
                    def _(t, c=c, g=g, pv=pb6_free):
                        t.wait_ge(sh("ss"), sq_mk[c % 2])
                        if c == 0:
                            wt(t, "sv", pv)
                        inc(t.matmul(pb[6][:, 0:TG], c_ones[:],
                                     stp[c % 2][:, 0:TG], start=(c == 0),
                                     stop=(c == NDC - 1)), "st")
                    mm_mk[c % 2] = cnt("st")
                done_mk = cnt("st")

                @blk.vector
                def _(v, g=g, tf=tmp_free):
                    v.wait_ge(sh("st"), done_mk)
                    wt(v, "ss", tf)
                    if g == 0:
                        wt(v, "ss", e_ss)
                    inc(v.tensor_scalar(tmp_b[:, 0:TG], pb[6][:, 0:TG],
                                        1.0 / D, EPS_RMS, op0=ALU.mult,
                                        op1=ALU.add), "sv")
                ts_mk = cnt("sv")
                pb6_free = ts_mk

                @blk.vector
                def _(v, g=g):
                    v.wait_ge(sh("sv"), ts_mk)
                    inc(v.reciprocal(tmp_b[:, 0:TG], tmp_b[:, 0:TG]), "sv")
                rc_mk = cnt("sv")

                @blk.scalar
                def _(s, g=g, rf=rstd_free):
                    s.wait_ge(sh("sv"), rc_mk)
                    wt(s, "sv", rf)
                    inc(s.activation(rstd_b[:, 0:TG], tmp_b[:, 0:TG],
                                     AF.Sqrt), "ss")
                sqrt_mk = cnt("ss")
                tmp_free = sqrt_mk

                for c in range(NDC):
                    @blk.vector
                    def _(v, c=c, g=g):
                        if c == 0:
                            v.wait_ge(sh("ss"), sqrt_mk)
                        if g == 0 and c == 0 and rfree is not None:
                            wt(v, rfree[0], rfree[1])
                        wt(v, "sv", lo_mk[c % 2])
                        inc(v.scalar_tensor_tensor(
                            stp[c % 2][:, 0:TG], X[c][:, gb:gb + TG],
                            gamma[:, c:c + 1], rstd_b[:, 0:TG],
                            op0=ALU.mult, op1=ALU.mult), "sv")
                    ap_mk = cnt("sv")

                    @blk.vector
                    def _(v, c=c, g=g, am=ap_mk):
                        v.wait_ge(sh("sv"), am)
                        inc(v.tensor_copy(r_hi(c)[:, gb:gb + TG],
                                          stp[c % 2][:, 0:TG]), "sv")
                    hi_mk[c % 2] = cnt("sv")

                    @blk.vector
                    def _(v, c=c, g=g, im=hi_mk[c % 2]):
                        v.wait_ge(sh("sv"), im)
                        inc(v.scalar_tensor_tensor(
                            r_lo(c)[:, gb:gb + TG], stp[c % 2][:, 0:TG], 1.0,
                            r_hi(c)[:, gb:gb + TG], op0=ALU.mult,
                            op1=ALU.subtract), "sv")
                    lo_mk[c % 2] = cnt("sv")
                rstd_free = cnt("sv")
                rdone.append(cnt("sv"))
            return rdone

        def norm_ln(blk, xmk=None, entry_sv=None):
            adone = []
            dx_all = cnt("dx")
            e_sv = cnt("sv") if entry_sv is None else entry_sv
            e_ss = cnt("ss")
            sq_mk = [0, 0]
            mm_mk = [0, 0]
            pb6_free = 0
            pb7_free = 0
            tmp_free = 0
            mean_free = 0
            rstd_free = 0
            a1_mk = [0, 0]
            a2_mk = [0, 0]
            for g in range(NTG):
                gb = g * TG
                for c in range(NDC):
                    @blk.scalar
                    def _(s, c=c, g=g):
                        if g == 0 and c == 0:
                            s.wait_ge(sh("dx"), dx_all)
                            wt(s, "sv", e_sv)
                        if xmk is not None:
                            for s_, v_ in xmk(c, g):
                                wt(s, s_, v_)
                        wt(s, "st", mm_mk[c % 2])
                        wt(s, "sv", a2_mk[c % 2])
                        inc(s.activation(stp[c % 2][:, 0:TG],
                                         X[c][:, gb:gb + TG], AF.Square), "ss")
                    sq_mk[c % 2] = cnt("ss")

                    @blk.tensor
                    def _(t, c=c, g=g, pv=pb6_free):
                        t.wait_ge(sh("ss"), sq_mk[c % 2])
                        if c == 0:
                            wt(t, "sv", pv)
                        inc(t.matmul(pb[6][:, 0:TG], c_ones[:],
                                     stp[c % 2][:, 0:TG], start=(c == 0),
                                     stop=(c == NDC - 1)), "st")
                    mm_mk[c % 2] = cnt("st")

                @blk.tensor
                def _(t, g=g, pm=pb7_free):
                    wt(t, "sv", pm)
                    if xmk is not None:
                        for c in range(NDC):
                            for s_, v_ in xmk(c, g):
                                wt(t, s_, v_)
                    mm = None
                    for c in range(NDC):
                        mm = t.matmul(pb[7][:, 0:TG], c_ones[:],
                                      X[c][:, gb:gb + TG], start=(c == 0),
                                      stop=(c == NDC - 1))
                    inc(mm, "st")
                done_mk = cnt("st")

                @blk.vector
                def _(v, g=g, mf=mean_free):
                    v.wait_ge(sh("st"), done_mk)
                    wt(v, "sv", mf)
                    inc(v.tensor_scalar(mean_b[:, 0:TG], pb[7][:, 0:TG],
                                        1.0 / D, None, op0=ALU.mult), "sv")
                mn_mk = cnt("sv")
                pb7_free = mn_mk

                @blk.scalar
                def _(s, g=g, tf=tmp_free):
                    s.wait_ge(sh("sv"), mn_mk)
                    wt(s, "ss", tf)
                    if g == 0:
                        wt(s, "ss", e_ss)
                    inc(s.activation(tmp_b[:, 0:TG], mean_b[:, 0:TG],
                                     AF.Square), "ss")
                m2_mk = cnt("ss")

                @blk.vector
                def _(v, g=g):
                    v.wait_ge(sh("ss"), m2_mk)
                    inc(v.scalar_tensor_tensor(
                        tmp_b[:, 0:TG], pb[6][:, 0:TG], 1.0 / D,
                        tmp_b[:, 0:TG], op0=ALU.mult, op1=ALU.subtract), "sv")
                ve_mk = cnt("sv")
                pb6_free = ve_mk

                @blk.vector
                def _(v, g=g):
                    v.wait_ge(sh("sv"), ve_mk)
                    inc(v.tensor_scalar(tmp_b[:, 0:TG], tmp_b[:, 0:TG], 1.0,
                                        EPS_LN, op0=ALU.mult, op1=ALU.add),
                        "sv")
                ep_mk = cnt("sv")

                @blk.vector
                def _(v, g=g):
                    v.wait_ge(sh("sv"), ep_mk)
                    inc(v.reciprocal(tmp_b[:, 0:TG], tmp_b[:, 0:TG]), "sv")
                rc_mk = cnt("sv")

                @blk.scalar
                def _(s, g=g, rf=rstd_free):
                    s.wait_ge(sh("sv"), rc_mk)
                    wt(s, "sv", rf)
                    inc(s.activation(rstd_b[:, 0:TG], tmp_b[:, 0:TG],
                                     AF.Sqrt), "ss")
                sqrt_mk = cnt("ss")
                tmp_free = sqrt_mk

                for c in range(NDC):
                    @blk.vector
                    def _(v, c=c, g=g):
                        if c == 0:
                            v.wait_ge(sh("ss"), sqrt_mk)
                        wt(v, "sv", a2_mk[c % 2])
                        inc(v.scalar_tensor_tensor(
                            stp[c % 2][:, 0:TG], X[c][:, gb:gb + TG], 1.0,
                            mean_b[:, 0:TG], op0=ALU.mult,
                            op1=ALU.subtract), "sv")
                    a1_mk[c % 2] = cnt("sv")

                    @blk.vector
                    def _(v, c=c, g=g, am=a1_mk[c % 2]):
                        v.wait_ge(sh("sv"), am)
                        inc(v.tensor_tensor(X[c][:, gb:gb + TG],
                                            stp[c % 2][:, 0:TG],
                                            rstd_b[:, 0:TG], op=ALU.mult),
                            "sv")
                    a2_mk[c % 2] = cnt("sv")
                mean_free = cnt("sv")
                rstd_free = cnt("sv")
                adone.append(cnt("sv"))
            return adone

        def emit_loadx(blk, h):
            @blk.sync
            def _(sync):
                wt(sync, "do")
                for c in range(NDC):
                    inc(sync.dma_start(
                        out=X[c][:],
                        in_=xt[c * 128:(c + 1) * 128, h * H:(h + 1) * H]),
                        "dx", 16)

        def emit_mixmm(blk, h, rdone):
            srcs = []
            for oc in range(NDC):
                for wi in range(3):
                    j = oc * 3 + wi
                    srcs.append(wgvd[j][:, 0:512])
                    srcs.append(wgvd[j][:, 512:1024])
            fs = FStream(blk, srcs)
            bank_mk = {}
            h_strip = [stp[0], stp[1]]
            scan_mk = [0, 0]
            add_mk = [rdone[-1], rdone[-1]]
            ep_mk = rdone[-1]
            xadd = {}
            for oc in range(NDC):
                for wi in range(3):
                    ju = (oc * 3 + wi) * 2
                    for g in range(NTG):
                        bank = pb[wi + 3 * (g % 2)]

                        @blk.tensor
                        def _(t, oc=oc, g=g, wi=wi, ju=ju, bank=bank):
                            fs.twait(t, ju)
                            t.wait_ge(sh("sv"), rdone[g])
                            if (wi, g % 2) in bank_mk:
                                t.wait_ge(sh("ss"), bank_mk[(wi, g % 2)])
                            mm3h(t, bank[:, 0:TG], fs.buf(ju),
                                 lambda kc: r_hi(kc)[:, g * TG:(g + 1) * TG],
                                 lambda kc: r_lo(kc)[:, g * TG:(g + 1) * TG],
                                 0, 4, True, False)
                            fs.twait(t, ju + 1)
                            mm = mm3h(t, bank[:, 0:TG], fs.buf(ju + 1),
                                      lambda kc: r_hi(kc)[:, g * TG:(g + 1) * TG],
                                      lambda kc: r_lo(kc)[:, g * TG:(g + 1) * TG],
                                      4, 4, False, True)
                            inc(mm, "st")
                    fs.done(ju)
                    fs.done(ju + 1)
                mm_last = cnt("st")

                for g in range(NTG):
                    gb = g * TG
                    pg_ = pb[0 + 3 * (g % 2)]
                    pv_ = pb[1 + 3 * (g % 2)]
                    pd_ = pb[2 + 3 * (g % 2)]

                    @blk.scalar
                    def _(s, oc=oc, g=g, pg_=pg_, pv_=pv_, pd_=pd_,
                          em=ep_mk, ml=mm_last):
                        s.wait_ge(sh("st"), ml)
                        wt(s, "sv", em)
                        inc(s.activation(rstd_b[:, 0:TG], pg_[:, 0:TG],
                                         AF.Sigmoid,
                                         bias=c_bg[:, oc:oc + 1]), "ss")
                        inc(s.activation(mean_b[:, 0:TG], pv_[:, 0:TG],
                                         AF.Tanh,
                                         bias=c_bv[:, oc:oc + 1]), "ss")
                        inc(s.activation(tmp_b[:, 0:TG], pd_[:, 0:TG],
                                         AF.Sigmoid,
                                         bias=c_bd[:, oc:oc + 1]), "ss")
                    act_mk = cnt("ss")
                    bank_mk[(0, g % 2)] = act_mk
                    bank_mk[(1, g % 2)] = act_mk
                    bank_mk[(2, g % 2)] = act_mk

                    @blk.vector
                    def _(v, oc=oc, g=g, am=act_mk, em=ep_mk):
                        v.wait_ge(sh("ss"), am)
                        wt(v, "sv", em)
                        inc(v.tensor_tensor(rstd_b[:, 0:TG], rstd_b[:, 0:TG],
                                            mean_b[:, 0:TG], op=ALU.mult),
                            "sv")
                        inc(v.tensor_scalar(tmp_b[:, 0:TG], tmp_b[:, 0:TG],
                                            0.998, 0.001, op0=ALU.mult,
                                            op1=ALU.add), "sv")
                    xsa_mk = cnt("sv")

                    @blk.vector
                    def _(v, oc=oc, g=g, xm=xsa_mk):
                        v.wait_ge(sh("sv"), xm)
                        wt(v, "sv", add_mk[g % 2])
                        if g == 0:
                            init = scn_c[:, oc:oc + 1]
                        else:
                            init = h_strip[(g - 1) % 2][:, TG - 1:TG]
                        inc(v.tensor_tensor_scan(
                            h_strip[g % 2][:, 0:TG], tmp_b[:, 0:TG],
                            rstd_b[:, 0:TG], init, op0=ALU.mult,
                            op1=ALU.add), "sv")
                    scan_mk[g % 2] = cnt("sv")
                    ep_mk = scan_mk[g % 2]

                    @blk.vector
                    def _(v, oc=oc, g=g, gb=gb, sm=scan_mk[g % 2]):
                        v.wait_ge(sh("sv"), sm)
                        inc(v.tensor_tensor(X[oc][:, gb:gb + TG],
                                            X[oc][:, gb:gb + TG],
                                            h_strip[g % 2][:, 0:TG],
                                            op=ALU.add), "sv")
                        if g == NTG - 1:
                            inc(v.tensor_copy(
                                scn_c[:, oc:oc + 1],
                                h_strip[g % 2][:, TG - 1:TG]), "sv")
                    add_mk[g % 2] = cnt("sv")
                    xadd[(oc, g)] = add_mk[g % 2]
            return xadd, cnt("st")

        def emit_atv(blk, h):
            vr_mk = []
            for kc in range(NDC):
                for hw in range(2):
                    j = kc * 2 + hw
                    free = [("sv", vr_mk[j - 2][1])] if j >= 2 else []
                    kl = 512

                    @blk.sync
                    def _(sync, kc=kc, hw=hw, free=free):
                        for s_, v_ in free:
                            wt(sync, s_, v_)
                        inc(sync.dma_start(
                            out=wstg[j % 2][:, 0:512],
                            in_=wvr[kc][:, hw * 512:(hw + 1) * 512]),
                            "dw%d" % (j % 2), 16)
                    dmk = cnt("dw%d" % (j % 2))

                    @blk.vector
                    def _(v, kc=kc, hw=hw, j=j, dmk=dmk):
                        v.wait_ge(sh("dw%d" % (j % 2)), dmk)
                        inc(v.tensor_copy(
                            vr_hi(kc)[:, hw * 512:(hw + 1) * 512],
                            wstg[j % 2][:, 0:512]), "sv")
                    hi_mk = cnt("sv")

                    @blk.vector
                    def _(v, kc=kc, hw=hw, j=j, hm=hi_mk):
                        v.wait_ge(sh("sv"), hm)
                        inc(v.scalar_tensor_tensor(
                            vr_lo(kc)[:, hw * 512:(hw + 1) * 512],
                            wstg[j % 2][:, 0:512], 1.0,
                            vr_hi(kc)[:, hw * 512:(hw + 1) * 512],
                            op0=ALU.mult, op1=ALU.subtract), "sv")
                    vr_mk.append((hi_mk, cnt("sv")))

            vr_v = vr_mk[-1][1]
            ev_mk = {}
            smk_d = {}
            for tc in range(NTC):
                for vh in range(2):
                    u = tc * 2 + vh
                    bank = pb[u % 4]

                    @blk.tensor
                    def _(t, tc=tc, vh=vh, u=u, bank=bank):
                        t.wait_ge(sh("sv"), vr_v)
                        if u >= 4:
                            t.wait_ge(sh("sv"), ev_mk[u - 4])
                        mm = None
                        for kc in range(NDC):
                            lh = r_hi(kc)[:, tc * 128:(tc + 1) * 128]
                            ll = r_lo(kc)[:, tc * 128:(tc + 1) * 128]
                            rh = vr_hi(kc)[:, vh * 512:(vh + 1) * 512]
                            rl = vr_lo(kc)[:, vh * 512:(vh + 1) * 512]
                            t.matmul(bank[:], lh, rh, start=(kc == 0),
                                     stop=False)
                            t.matmul(bank[:], lh, rl, start=False,
                                     stop=False)
                            mm = t.matmul(bank[:], ll, rh, start=False,
                                          stop=(kc == NDC - 1))
                        inc(mm, "st")
                    t_mk = cnt("st")

                    @blk.vector
                    def _(v, tc=tc, vh=vh, bank=bank, tm=t_mk, u=u):
                        v.wait_ge(sh("st"), tm)
                        inc(v.tensor_copy(
                            v_hi(tc + 1)[:, vh * 512:(vh + 1) * 512],
                            bank[:]), "sv")
                    s_mk = cnt("sv")
                    smk_d[u] = s_mk

                    @blk.vector
                    def _(v, tc=tc, vh=vh, u=u, sm=s_mk, bank=bank):
                        v.wait_ge(sh("sv"), sm)
                        inc(v.scalar_tensor_tensor(
                            v_lo(tc + 1)[:, vh * 512:(vh + 1) * 512],
                            bank[:], 1.0,
                            v_hi(tc + 1)[:, vh * 512:(vh + 1) * 512],
                            op0=ALU.mult, op1=ALU.subtract), "sv")
                    ev_mk[u] = cnt("sv")


        def emit_qk(blk, h, part, qfree=None):
            srcs = []
            for oc in range(NDC):
                srcs.append(wqk[part * 8 + oc][:, 0:512])
                srcs.append(wqk[part * 8 + oc][:, 512:1024])
            fs = FStream(blk, srcs)
            scale = 0.125 if part == 0 else 1.0
            ev_mk = {}
            smk_d = {}
            for oc in range(NDC):
                for g in range(NTG):
                    u = oc * NTG + g
                    bank = pb[u % 4]

                    @blk.tensor
                    def _(t, oc=oc, g=g, u=u, bank=bank):
                        fs.twait(t, 2 * oc)
                        if u >= 4:
                            t.wait_ge(sh("sv"), ev_mk[u - 4])
                        mm3h(t, bank[:, 0:TG], fs.buf(2 * oc),
                             lambda kc: r_hi(kc)[:, g * TG:(g + 1) * TG],
                             lambda kc: r_lo(kc)[:, g * TG:(g + 1) * TG],
                             0, 4, True, False)
                        fs.twait(t, 2 * oc + 1)
                        mm = mm3h(t, bank[:, 0:TG], fs.buf(2 * oc + 1),
                                  lambda kc: r_hi(kc)[:, g * TG:(g + 1) * TG],
                                  lambda kc: r_lo(kc)[:, g * TG:(g + 1) * TG],
                                  4, 4, False, True)
                        inc(mm, "st")
                    if g == NTG - 1:
                        fs.done(2 * oc)
                        fs.done(2 * oc + 1)
                    t_mk = cnt("st")

                    gb = g * TG
                    if part == 0:
                        dhi = q_hi(oc)[:, gb:gb + TG]
                        dlo = q_lo(oc)[:, gb:gb + TG]
                    else:
                        dhi = k_hi(oc)[:, gb:gb + TG]
                        dlo = k_lo(oc)[:, gb:gb + TG]

                    @blk.vector
                    def _(v, bank=bank, dhi=dhi, u=u, tm=t_mk):
                        v.wait_ge(sh("st"), tm)
                        inc(v.tensor_scalar(dhi, bank[:, 0:TG],
                                            scale, None,
                                            op0=ALU.mult), "sv")
                    s_mk = cnt("sv")
                    smk_d[u] = s_mk

                    @blk.vector
                    def _(v, u=u, dhi=dhi, dlo=dlo, sm=s_mk,
                          bank=bank):
                        v.wait_ge(sh("sv"), sm)
                        inc(v.scalar_tensor_tensor(
                            dlo, bank[:, 0:TG], scale, dhi,
                            op0=ALU.mult, op1=ALU.subtract), "sv")
                    ev_mk[u] = cnt("sv")

        def emit_atloop(blk, h):
            entry_sv = cnt("sv")
            entry_st = cnt("st")
            its = [(tc, hp) for tc in range(NTC) for hp in range(NH // 2)]
            NIT = len(its)
            t_sc = {}
            v_nm = {}
            s_exp = {}
            v_pn = {}
            t_pt = {}
            s_pt = {}
            v_ptl = {}
            t_av = {}
            s_ao = {}
            v_aol = {}
            sps = [stp[0], stp[1], mean_b]

            def pt2_hi(slot):
                return av(PTB + slot * 1024, 512)

            def pt2_lo(slot):
                return av(PTB + slot * 1024 + 512, 512)

            def stage1(it):
                tc, hp = its[it]
                qc = hp
                sl = it % 2
                pS = pb[sl]
                spv = sps[it % 3]
                moff = 0 if (h == 0 and tc == 0) else 256

                @blk.tensor
                def _(t, tc=tc, qc=qc, pS=pS, moff=moff, it=it):
                    wt(t, "dx")
                    if it == 0:
                        t.wait_ge(sh("sv"), entry_sv)
                    if it >= 2:
                        t.wait_ge(sh("sv"), v_nm[it - 2])
                        t.wait_ge(sh("ss"), s_exp[it - 2])
                    for hx in range(2):
                        po = 64 * hx
                        n0 = 256 * hx
                        t.matmul(pS[:, n0:n0 + 256], c_idb[:],
                                 c_mask[:, moff:moff + 256],
                                 start=True, stop=False)
                        qh = q_hi(qc)[po:po + 64, tc * 128:(tc + 1) * 128]
                        ql = q_lo(qc)[po:po + 64, tc * 128:(tc + 1) * 128]
                        if tc == 0:
                            kparts = [(pS[:, n0:n0 + 128],
                                       kp_hi(qc)[po:po + 64, :],
                                       kp_lo(qc)[po:po + 64, :]),
                                      (pS[:, n0 + 128:n0 + 256],
                                       k_hi(qc)[po:po + 64, 0:128],
                                       k_lo(qc)[po:po + 64, 0:128])]
                        else:
                            kparts = [(pS[:, n0:n0 + 256],
                                       k_hi(qc)[po:po + 64,
                                                (tc - 1) * 128:(tc + 1) * 128],
                                       k_lo(qc)[po:po + 64,
                                                (tc - 1) * 128:(tc + 1) * 128])]
                        mm = None
                        for pi, (pdst, kh, kl) in enumerate(kparts):
                            last = (pi == len(kparts) - 1)
                            t.matmul(pdst, qh, kh, start=False, stop=False)
                            t.matmul(pdst, qh, kl, start=False, stop=False)
                            mm = t.matmul(pdst, ql, kh, start=False,
                                          stop=last)
                    inc(mm, "st")
                t_sc[it] = cnt("st")

                @blk.vector
                def _(v, sl=sl, pS=pS, it=it):
                    v.wait_ge(sh("st"), t_sc[it])
                    inc(v.reduce_max(nmb[:, 2 * sl:2 * sl + 1],
                                     pS[:, 0:256], axis=AX.X,
                                     negate=True), "sv")
                    inc(v.reduce_max(nmb[:, 2 * sl + 1:2 * sl + 2],
                                     pS[:, 256:512], axis=AX.X,
                                     negate=True), "sv")
                v_nm[it] = cnt("sv")

                @blk.scalar
                def _(s, sl=sl, pS=pS, spv=spv, it=it):
                    s.wait_ge(sh("sv"), v_nm[it])
                    if it >= 3:
                        s.wait_ge(sh("sv"), v_pn[it - 3])
                        s.wait_ge(sh("st"), t_pt[it - 3])
                    inc(s.activation(spv[:, 0:256], pS[:, 0:256], AF.Exp,
                                     bias=nmb[:, 2 * sl:2 * sl + 1]),
                        "ss")
                    inc(s.activation(spv[:, 256:512], pS[:, 256:512],
                                     AF.Exp,
                                     bias=nmb[:, 2 * sl + 1:2 * sl + 2]),
                        "ss")
                s_exp[it] = cnt("ss")

                @blk.vector
                def _(v, sl=sl, spv=spv, it=it):
                    v.wait_ge(sh("ss"), s_exp[it])
                    inc(v.reduce_sum(smb[:, 2 * sl:2 * sl + 1],
                                     spv[:, 0:256], axis=AX.X), "sv")
                    inc(v.reduce_sum(smb[:, 2 * sl + 1:2 * sl + 2],
                                     spv[:, 256:512], axis=AX.X), "sv")
                sm_ = cnt("sv")

                @blk.vector
                def _(v, sl=sl, it=it, sm_=sm_):
                    v.wait_ge(sh("sv"), sm_)
                    inc(v.reciprocal(rrb[:, 2 * sl:2 * sl + 2],
                                     smb[:, 2 * sl:2 * sl + 2]), "sv")
                rc = cnt("sv")

                @blk.vector
                def _(v, sl=sl, spv=spv, it=it, rc=rc):
                    v.wait_ge(sh("sv"), rc)
                    inc(v.tensor_scalar(spv[:, 0:256], spv[:, 0:256],
                                        rrb[:, 2 * sl:2 * sl + 1], None,
                                        op0=ALU.mult), "sv")
                    inc(v.tensor_scalar(spv[:, 256:512], spv[:, 256:512],
                                        rrb[:, 2 * sl + 1:2 * sl + 2],
                                        None, op0=ALU.mult), "sv")
                v_pn[it] = cnt("sv")

            def stage2(it):
                sl = it % 2
                pPT = pb[2 + sl]
                spv = sps[it % 3]

                @blk.tensor
                def _(t, pPT=pPT, spv=spv, it=it):
                    t.wait_ge(sh("sv"), v_pn[it])
                    if it >= 2:
                        t.wait_ge(sh("sv"), v_ptl[it - 2])
                    for q4 in range(4):
                        mm = t.transpose(pPT[:, q4 * 128:(q4 + 1) * 128],
                                         spv[:, q4 * 128:(q4 + 1) * 128],
                                         c_idf[:])
                    inc(mm, "st")
                t_pt[it] = cnt("st")

                @blk.vector
                def _(v, sl=sl, pPT=pPT, it=it):
                    v.wait_ge(sh("st"), t_pt[it])
                    if it >= 2:
                        v.wait_ge(sh("st"), t_av[it - 2])
                    inc(v.tensor_copy(pt2_hi(sl), pPT[:, 0:512]), "sv")
                s_pt[it] = cnt("sv")

                @blk.vector
                def _(v, sl=sl, pPT=pPT, it=it):
                    v.wait_ge(sh("sv"), s_pt[it])
                    inc(v.scalar_tensor_tensor(
                        pt2_lo(sl), pPT[:, 0:512], 1.0, pt2_hi(sl),
                        op0=ALU.mult, op1=ALU.subtract), "sv")
                v_ptl[it] = cnt("sv")

            def stage3(it):
                tc, hp = its[it]
                qc = hp
                sl = it % 2
                pA = pb[4 + sl]

                @blk.tensor
                def _(t, tc=tc, hp=hp, sl=sl, pA=pA, it=it):
                    t.wait_ge(sh("sv"), v_ptl[it])
                    if it >= 2:
                        t.wait_ge(sh("sv"), v_aol[it - 2])
                    mm = None
                    for hx in range(2):
                        hh = 2 * hp + hx
                        po = 64 * hx
                        n0 = 256 * hx
                        tp = (0, 64) if po else None
                        for j, vt in ((0, tc), (1, tc + 1)):
                            if vt == 0:
                                vh_ = vp_hi()[:, hh * 64:(hh + 1) * 64]
                                vl_ = vp_lo()[:, hh * 64:(hh + 1) * 64]
                            else:
                                vh_ = v_hi(vt)[:, hh * 64:(hh + 1) * 64]
                                vl_ = v_lo(vt)[:, hh * 64:(hh + 1) * 64]
                            ph = pt2_hi(sl)[:, n0 + j * 128:n0 + (j + 1) * 128]
                            pl = pt2_lo(sl)[:, n0 + j * 128:n0 + (j + 1) * 128]
                            t.matmul(pA[po:po + 64, 0:128], vh_, ph,
                                     start=(j == 0), stop=False,
                                     tile_position=tp)
                            t.matmul(pA[po:po + 64, 0:128], vh_, pl,
                                     start=False, stop=False,
                                     tile_position=tp)
                            mm = t.matmul(pA[po:po + 64, 0:128], vl_, ph,
                                          start=False, stop=(j == 1),
                                          tile_position=tp)
                    inc(mm, "st")
                t_av[it] = cnt("st")

                @blk.vector
                def _(v, tc=tc, qc=qc, pA=pA, it=it):
                    v.wait_ge(sh("st"), t_av[it])
                    if it == 0:
                        v.wait_ge(sh("st"), entry_st)
                    if it >= 2:
                        v.wait_ge(sh("sv"), v_aol[it - 2])
                    dhi = r_hi(qc)[:, tc * 128:(tc + 1) * 128]
                    inc(v.tensor_copy(dhi, pA[:, 0:128]), "sv")
                s_ao[it] = cnt("sv")

                @blk.vector
                def _(v, tc=tc, qc=qc, pA=pA, it=it):
                    v.wait_ge(sh("sv"), s_ao[it])
                    dhi = r_hi(qc)[:, tc * 128:(tc + 1) * 128]
                    dlo = r_lo(qc)[:, tc * 128:(tc + 1) * 128]
                    inc(v.scalar_tensor_tensor(
                        dlo, pA[:, 0:128], 1.0, dhi,
                        op0=ALU.mult, op1=ALU.subtract), "sv")
                v_aol[it] = cnt("sv")

            for step in range(NIT + 3):
                if step < NIT:
                    stage1(step)
                if 0 <= step - 2 < NIT:
                    stage2(step - 2)
                if 0 <= step - 3 < NIT:
                    stage3(step - 3)

            @blk.vector
            def _(v):
                wt(v, "st")
                for c in range(NDC):
                    inc(v.tensor_copy(kp_hi(c),
                                      k_hi(c)[:, H - 128:H]), "sv")
                    inc(v.tensor_copy(kp_lo(c),
                                      k_lo(c)[:, H - 128:H]), "sv")
                inc(v.tensor_copy(vp_hi(), v_hi(NTC)), "sv")
                inc(v.tensor_copy(vp_lo(), v_lo(NTC)), "sv")
            tcs_per_g = TG // 128
            aog = []
            for g in range(NTG):
                last_it = ((g + 1) * tcs_per_g - 1) * (NH // 2) + NH // 2 - 1
                aog.append(v_aol[last_it])
            return aog


        def emit_wo(blk, h, aog):
            srcs = []
            for oc in range(NDC):
                srcs.append(wo_[oc][:, 0:512])
                srcs.append(wo_[oc][:, 512:1024])
            fs = FStream(blk, srcs)
            add_mk = {}
            wadd = {}
            for oc in range(NDC):
                for g in range(NTG):
                    u = oc * NTG + g
                    bank = pb[u % 4]

                    @blk.tensor
                    def _(t, oc=oc, g=g, u=u, bank=bank):
                        fs.twait(t, 2 * oc)
                        if u >= 4:
                            t.wait_ge(sh("sv"), add_mk[u - 4])
                        mm3h(t, bank[:, 0:TG], fs.buf(2 * oc),
                             lambda kc: r_hi(kc)[:, g * TG:(g + 1) * TG],
                             lambda kc: r_lo(kc)[:, g * TG:(g + 1) * TG],
                             0, 4, True, False)
                        fs.twait(t, 2 * oc + 1)
                        mm = mm3h(t, bank[:, 0:TG], fs.buf(2 * oc + 1),
                                  lambda kc: r_hi(kc)[:, g * TG:(g + 1) * TG],
                                  lambda kc: r_lo(kc)[:, g * TG:(g + 1) * TG],
                                  4, 4, False, True)
                        inc(mm, "st")
                    if g == NTG - 1:
                        fs.done(2 * oc)
                        fs.done(2 * oc + 1)
                    t_mk = cnt("st")

                    @blk.vector
                    def _(v, oc=oc, g=g, bank=bank, tm=t_mk):
                        v.wait_ge(sh("st"), tm)
                        gb = g * TG
                        inc(v.scalar_tensor_tensor(
                            X[oc][:, gb:gb + TG], bank[:, 0:TG], 1.0,
                            X[oc][:, gb:gb + TG], op0=ALU.mult,
                            op1=ALU.add), "sv")
                    add_mk[u] = cnt("sv")
                    wadd[(oc, g)] = add_mk[u]
            return wadd, cnt("st")

        def emit_glu_g(blk, h, g):
            gb = g * TG
            srcs = []
            for f in range(NFC):
                for wi in range(2):
                    srcs.append(wgu[2 * f + wi][:, 0:512])
                    srcs.append(wgu[2 * f + wi][:, 512:1024])
            fs = FStream(blk, srcs)
            hg_mk = {}
            sil_mk_d = {}
            hf_mk_d = {}
            for f in range(NFC):
                sbA = (tmp_b, rstd_b) if f % 2 == 0 else (stp[0], stp[1])
                for wi in range(2):
                    ju = (2 * f + wi) * 2
                    bank = pb[wi + 2 * (f % 2)]

                    @blk.tensor
                    def _(t, f=f, wi=wi, ju=ju, bank=bank):
                        fs.twait(t, ju)
                        if f >= 2 and wi == 0:
                            t.wait_ge(sh("ss"), sil_mk_d[f - 2])
                        if f >= 2 and wi == 1:
                            t.wait_ge(sh("sv"), hf_mk_d[f - 2])
                        mm3h(t, bank[:, 0:TG], fs.buf(ju),
                             lambda kc: r_hi(kc)[:, gb:gb + TG],
                             lambda kc: r_lo(kc)[:, gb:gb + TG],
                             0, 4, True, False)
                        fs.twait(t, ju + 1)
                        mm = mm3h(t, bank[:, 0:TG], fs.buf(ju + 1),
                                  lambda kc: r_hi(kc)[:, gb:gb + TG],
                                  lambda kc: r_lo(kc)[:, gb:gb + TG],
                                  4, 4, False, True)
                        inc(mm, "st")
                    fs.done(ju)
                    fs.done(ju + 1)
                t_mk = cnt("st")
                pg_ = pb[0 + 2 * (f % 2)]
                pu_ = pb[1 + 2 * (f % 2)]

                @blk.scalar
                def _(s, f=f, pg_=pg_, tm=t_mk, sbA=sbA,
                      sf=hf_mk_d.get(f - 2, 0)):
                    s.wait_ge(sh("st"), tm)
                    wt(s, "sv", sf)
                    inc(s.activation(sbA[0][:, 0:TG], pg_[:, 0:TG],
                                     AF.Silu), "ss")
                sil_mk_d[f] = cnt("ss")

                @blk.vector
                def _(v, f=f, pu_=pu_, sm=sil_mk_d[f], sbA=sbA,
                      hf=hg_mk.get(f - 2, 0)):
                    v.wait_ge(sh("ss"), sm)
                    wt(v, "sv", hf)
                    inc(v.scalar_tensor_tensor(
                        sbA[1][:, 0:TG], pu_[:, 0:TG], 1.0,
                        sbA[0][:, 0:TG], op0=ALU.mult, op1=ALU.mult),
                        "sv")
                hf_mk_d[f] = cnt("sv")

                @blk.vector
                def _(v, f=f, hm=hf_mk_d[f], sbA=sbA):
                    v.wait_ge(sh("sv"), hm)
                    inc(v.tensor_copy(ha_hi(f)[:, 0:TG],
                                      sbA[1][:, 0:TG]), "sv")
                hi_mk = cnt("sv")

                @blk.vector
                def _(v, f=f, im=hi_mk, sbA=sbA):
                    v.wait_ge(sh("sv"), im)
                    inc(v.scalar_tensor_tensor(
                        ha_lo(f)[:, 0:TG], sbA[1][:, 0:TG], 1.0,
                        ha_hi(f)[:, 0:TG], op0=ALU.mult,
                        op1=ALU.subtract), "sv")
                hg_mk[f] = cnt("sv")

            srcs2 = []
            for oc in range(NDC):
                for q in range(8):
                    srcs2.append(wout[oc][:, q * 512:(q + 1) * 512])
            fs2 = FStream(blk, srcs2)
            add_mk = {}
            for oc in range(NDC):
                bank = pb[4 + oc % 2]
                for q in range(8):
                    ju = oc * 8 + q

                    @blk.tensor
                    def _(t, oc=oc, q=q, ju=ju, bank=bank):
                        fs2.twait(t, ju)
                        if q == 0:
                            t.wait_ge(sh("sv"), hg_mk[NFC - 1])
                            if oc >= 2:
                                t.wait_ge(sh("sv"), add_mk[oc - 2])
                        mm = mm3h(t, bank[:, 0:TG], fs2.buf(ju),
                                  lambda kc, q=q: ha_hi(q * 4 + kc)[:, 0:TG],
                                  lambda kc, q=q: ha_lo(q * 4 + kc)[:, 0:TG],
                                  0, 4, q == 0, q == 7)
                        inc(mm, "st")
                    fs2.done(ju)
                t_mk = cnt("st")

                @blk.vector
                def _(v, oc=oc, bank=bank, tm=t_mk):
                    v.wait_ge(sh("st"), tm)
                    inc(v.scalar_tensor_tensor(
                        X[oc][:, gb:gb + TG], bank[:, 0:TG], 1.0,
                        X[oc][:, gb:gb + TG], op0=ALU.mult,
                        op1=ALU.add), "sv")
                add_mk[oc] = cnt("sv")

            nj = len(srcs2)
            wfree = (fs2.split[nj - 2][1], fs2.split[nj - 1][1])
            return add_mk, cnt("st"), hg_mk[NFC - 1], wfree

        def emit_ln1add(blk, h, gadd_last, wfree):
            add_mk = {}
            hw_w = 512 if H >= 1024 else H // 2
            for c in range(NDC):
                for hw in range(2):
                    j = c * 2 + hw
                    par = j % 2
                    lastg = ((hw + 1) * hw_w - 1) // TG == NTG - 1

                    @blk.sync
                    def _(sync, c=c, hw=hw, par=par, j=j, hw_w=hw_w,
                          lastg=lastg):
                        if j < 2:
                            wt(sync, "sv", wfree[par])
                        if j >= 2:
                            wt(sync, "sv", add_mk[(j - 2) // 2, (j - 2) % 2])
                        inc(sync.dma_start(
                            out=wstg[par][:, 0:hw_w],
                            in_=xt[c * 128:(c + 1) * 128,
                                   h * H + hw * hw_w:
                                   h * H + (hw + 1) * hw_w]),
                            "dw%d" % par, 16)
                    dmk = cnt("dw%d" % par)

                    @blk.vector
                    def _(v, c=c, hw=hw, par=par, dmk=dmk, hw_w=hw_w,
                          lastg=lastg):
                        v.wait_ge(sh("dw%d" % par), dmk)
                        if lastg:
                            wt(v, "sv", gadd_last[c])
                        inc(v.tensor_tensor(
                            X[c][:, hw * hw_w:(hw + 1) * hw_w],
                            X[c][:, hw * hw_w:(hw + 1) * hw_w],
                            wstg[par][:, 0:hw_w], op=ALU.add), "sv")
                    add_mk[(c, hw)] = cnt("sv")
            return add_mk

        def emit_ffnA(blk, h, g):
            gb = g * TG
            for c in range(NDC):
                @blk.vector
                def _(v, c=c):
                    inc(v.tensor_copy(x4g_hi(c)[:, 0:TG],
                                      X[c][:, gb:gb + TG]), "sv")
                hm = cnt("sv")

                @blk.vector
                def _(v, c=c, hm=hm):
                    v.wait_ge(sh("sv"), hm)
                    inc(v.scalar_tensor_tensor(
                        x4g_lo(c)[:, 0:TG], X[c][:, gb:gb + TG], 1.0,
                        x4g_hi(c)[:, 0:TG], op0=ALU.mult,
                        op1=ALU.subtract), "sv")
            x4_v = cnt("sv")
            dg_mk = {}
            used_mk = {}
            ha_mk = {}
            nsl = 2 * NFC

            def tput(j):
                slot = j % 4
                par, off = slot // 2, (slot % 2) * D
                free = [("st", used_mk[j - 4])] if j >= 4 else []

                @blk.gpsimd
                def _(gp, j=j, par=par, off=off, free=free):
                    for s_, v_ in free:
                        wt(gp, s_, v_)
                    inc(gp.dma_start(out=tsl[par][:, off:off + D],
                                     in_=w12[j]), "dg%d" % slot, 16)
                dg_mk[j] = (par, off, slot, cnt("dg%d" % slot))

            for j in range(min(4, nsl)):
                tput(j)
            sil_mk_d = {}
            hf_mk_d = {}
            for f in range(NFC):
                sbA = (tmp_b, rstd_b) if f % 2 == 0 else (stp[0], stp[1])
                for wi in range(2):
                    j = 2 * f + wi
                    par, off, slot, dmk = dg_mk[j]
                    bank = pb[wi + 2 * (f % 2)]

                    @blk.tensor
                    def _(t, f=f, wi=wi, par=par, off=off, slot=slot,
                          dmk=dmk, bank=bank):
                        t.wait_ge(sh("dg%d" % slot), dmk)
                        if f == 0 and wi == 0:
                            t.wait_ge(sh("sv"), x4_v)
                        if f >= 2 and wi == 0:
                            t.wait_ge(sh("ss"), sil_mk_d[f - 2])
                        if f >= 2 and wi == 1:
                            t.wait_ge(sh("sv"), hf_mk_d[f - 2])
                        mm = None
                        for kc in range(NDC):
                            lh = tsl[par][:, off + kc * 128:
                                          off + (kc + 1) * 128]
                            t.matmul(bank[:, 0:TG], lh,
                                     x4g_hi(kc)[:, 0:TG],
                                     start=(kc == 0), stop=False)
                            mm = t.matmul(bank[:, 0:TG], lh,
                                          x4g_lo(kc)[:, 0:TG],
                                          start=False,
                                          stop=(kc == NDC - 1))
                        inc(mm, "st")
                    used_mk[j] = cnt("st")
                    if j + 4 < nsl:
                        tput(j + 4)
                t_mk = cnt("st")
                pg_ = pb[0 + 2 * (f % 2)]
                pu_ = pb[1 + 2 * (f % 2)]

                @blk.scalar
                def _(s, f=f, pg_=pg_, tm=t_mk, sbA=sbA,
                      sf=hf_mk_d.get(f - 2, 0)):
                    s.wait_ge(sh("st"), tm)
                    wt(s, "sv", sf)
                    inc(s.activation(sbA[0][:, 0:TG], pg_[:, 0:TG],
                                     AF.Silu, scale=c_s1[:, f:f + 1]),
                        "ss")
                sil_mk_d[f] = cnt("ss")

                @blk.vector
                def _(v, f=f, pu_=pu_, sm=sil_mk_d[f], sbA=sbA,
                      hf=ha_mk.get(f - 2, 0)):
                    v.wait_ge(sh("ss"), sm)
                    wt(v, "sv", hf)
                    inc(v.scalar_tensor_tensor(
                        sbA[1][:, 0:TG], pu_[:, 0:TG], c_s2[:, f:f + 1],
                        sbA[0][:, 0:TG], op0=ALU.mult, op1=ALU.mult),
                        "sv")
                hf_mk_d[f] = cnt("sv")

                @blk.vector
                def _(v, f=f, hm=hf_mk_d[f], sbA=sbA):
                    v.wait_ge(sh("sv"), hm)
                    inc(v.tensor_copy(ha_hi(f)[:, 0:TG],
                                      sbA[1][:, 0:TG]), "sv")
                hi_mk = cnt("sv")

                @blk.vector
                def _(v, f=f, im=hi_mk, sbA=sbA):
                    v.wait_ge(sh("sv"), im)
                    inc(v.scalar_tensor_tensor(
                        ha_lo(f)[:, 0:TG], sbA[1][:, 0:TG], 1.0,
                        ha_hi(f)[:, 0:TG], op0=ALU.mult,
                        op1=ALU.subtract), "sv")
                ha_mk[f] = cnt("sv")


        def emit_hstage(blk, wsrc, src_hi, src_lo, dst_hi, dst_lo, sc):
            dg_mk = {}
            used_mk = {}
            out_mk = {}

            def tput(u):
                par = u % 2
                free = [("st", used_mk[u - 2])] if u >= 2 else []

                @blk.gpsimd
                def _(gp, u=u, par=par, free=free):
                    for s_, v_ in free:
                        wt(gp, s_, v_)
                    f, hw = u // 2, u % 2
                    inc(gp.dma_start(
                        out=tsl[par][:],
                        in_=wsrc[f][:, hw * 2048:(hw + 1) * 2048]),
                        "dg%d" % par, 16)
                dg_mk[u] = cnt("dg%d" % (u % 2))

            tput(0)
            tput(1)
            for f in range(NFC):
                bank = pb[f % 3]
                for hw in range(2):
                    u = 2 * f + hw

                    @blk.tensor
                    def _(t, f=f, hw=hw, u=u, bank=bank):
                        t.wait_ge(sh("dg%d" % (u % 2)), dg_mk[u])
                        if hw == 0 and f >= 3:
                            t.wait_ge(sh("sv"), out_mk[f - 3])
                        mm = None
                        for kc in range(16):
                            kk = hw * 16 + kc
                            lh = tsl[u % 2][:, kc * 128:(kc + 1) * 128]
                            t.matmul(bank[:, 0:TG], lh,
                                     src_hi(kk)[:, 0:TG],
                                     start=(hw == 0 and kc == 0),
                                     stop=False)
                            mm = t.matmul(
                                bank[:, 0:TG], lh,
                                src_lo(kk)[:, 0:TG], start=False,
                                stop=(hw == 1 and kc == 15))
                        inc(mm, "st")
                    used_mk[u] = cnt("st")
                    if u + 2 < 2 * NFC:
                        tput(u + 2)
                t_mk = cnt("st")

                @blk.scalar
                def _(s, f=f, bank=bank, tm=t_mk):
                    s.wait_ge(sh("st"), tm)
                    if f >= 2:
                        wt(s, "sv", out_mk[f - 2])
                    inc(s.activation(dst_hi(f)[:, 0:TG],
                                     bank[:, 0:TG], AF.Gelu,
                                     scale=sc[:, f:f + 1]), "ss")
                    inc(s.activation(stp[f % 2][:, 0:TG],
                                     bank[:, 0:TG], AF.Gelu,
                                     scale=sc[:, f:f + 1]), "ss")
                s_mk = cnt("ss")

                @blk.vector
                def _(v, f=f, sm=s_mk):
                    v.wait_ge(sh("ss"), sm)
                    inc(v.scalar_tensor_tensor(
                        dst_lo(f)[:, 0:TG], stp[f % 2][:, 0:TG],
                        1.0, dst_hi(f)[:, 0:TG], op0=ALU.mult,
                        op1=ALU.subtract), "sv")
                out_mk[f] = cnt("sv")

        def emit_ffnD(blk, h, g):
            gb = g * TG
            dg_mk = {}
            used_mk = {}
            add_mk = {}

            def tput(u):
                par = u % 2
                free = [("st", used_mk[u - 2])] if u >= 2 else []

                @blk.gpsimd
                def _(gp, u=u, par=par, free=free):
                    for s_, v_ in free:
                        wt(gp, s_, v_)
                    oc, hw = u // 2, u % 2
                    inc(gp.dma_start(
                        out=tsl[par][:],
                        in_=w3_[oc][:, hw * 2048:(hw + 1) * 2048]),
                        "dg%d" % par, 16)
                dg_mk[u] = cnt("dg%d" % (u % 2))

            tput(0)
            tput(1)
            for oc in range(NDC):
                bank = pb[4 + oc % 2]
                for hw in range(2):
                    u = 2 * oc + hw

                    @blk.tensor
                    def _(t, oc=oc, hw=hw, u=u, bank=bank):
                        t.wait_ge(sh("dg%d" % (u % 2)), dg_mk[u])
                        if hw == 0 and oc >= 2:
                            t.wait_ge(sh("sv"), add_mk[oc - 2])
                        mm = None
                        for kc in range(16):
                            kk = hw * 16 + kc
                            lh = tsl[u % 2][:, kc * 128:(kc + 1) * 128]
                            t.matmul(bank[:, 0:TG], lh,
                                     ha_hi(kk)[:, 0:TG],
                                     start=(hw == 0 and kc == 0),
                                     stop=False)
                            mm = t.matmul(bank[:, 0:TG], lh,
                                          ha_lo(kk)[:, 0:TG],
                                          start=False,
                                          stop=(hw == 1 and kc == 15))
                        inc(mm, "st")
                    used_mk[u] = cnt("st")
                    if u + 2 < 2 * NDC:
                        tput(u + 2)
                t_mk = cnt("st")

                @blk.vector
                def _(v, oc=oc, bank=bank, tm=t_mk):
                    v.wait_ge(sh("st"), tm)
                    inc(v.scalar_tensor_tensor(
                        X[oc][:, gb:gb + TG], bank[:, 0:TG],
                        c_s3[:, oc:oc + 1], X[oc][:, gb:gb + TG],
                        op0=ALU.mult, op1=ALU.add), "sv")
                add_mk[oc] = cnt("sv")

            return add_mk, cnt("st")

        def dump_dbg():
            with nc.Block() as blk:
                @blk.sync
                def _(sync):
                    for c in range(NDC):
                        inc(sync.dma_start(
                            out=dbg[c * 128:(c + 1) * 128, :], in_=X[c][:]),
                            "do", 16)
                    wt(sync, "do")

        for h in range(2):
            with nc.named_scope(f"loadx{h}"), nc.Block() as blk:
                emit_loadx(blk, h)
                rdone = norm_rms(blk, c_gmix)

            with nc.named_scope(f"mixmm{h}"), nc.Block() as blk:
                xadd, mml = emit_mixmm(blk, h, rdone)
                rdone = norm_rms(
                    blk, c_gattn,
                    xmk=lambda c, g, xadd=xadd: [("sv", xadd[(c, g)])],
                    rfree=("st", mml))
            if stop_after == "mix" and h == 0:
                dump_dbg()
                return nc

            with nc.named_scope(f"attn{h}"), nc.Block() as blk:
                emit_atv(blk, h)
                emit_qk(blk, h, 1)
                emit_qk(blk, h, 0, qfree=("st", cnt("st")))
                aog = emit_atloop(blk, h)
                ao_sv = cnt("sv")
                wadd, wol = emit_wo(blk, h, aog)
                rdone = norm_rms(
                    blk, c_gffn,
                    xmk=lambda c, g, wadd=wadd: [("sv", wadd[(c, g)])],
                    rfree=("st", wol), entry_sv=ao_sv)
            if stop_after == "attn" and h == 0:
                dump_dbg()
                return nc

            gadd = None
            for g in range(NTG):
                last = (g == NTG - 1)
                with nc.named_scope(f"glumm{h}_{g}"), nc.Block() as blk:
                    gadd, glul, hg_last, wfree = emit_glu_g(blk, h, g)
                    if last:
                        lnadd = emit_ln1add(blk, h, gadd, wfree)
                        hw_w = 512 if H >= 1024 else H // 2

                        def ln1_xmk(c, g2, lnadd=lnadd, hw_w=hw_w):
                            ws = []
                            for hw in range(2):
                                t0, t1 = hw * hw_w, (hw + 1) * hw_w
                                if t0 < (g2 + 1) * TG and t1 > g2 * TG:
                                    ws.append(("sv", lnadd[(c, hw)]))
                            return ws
                        norm_ln(blk, xmk=ln1_xmk, entry_sv=hg_last)
            if stop_after == "glu" and h == 0:
                dump_dbg()
                return nc

            dadd = None
            for g in range(NTG):
                last = (g == NTG - 1)
                with nc.named_scope(f"ffnA{h}_{g}"), nc.Block() as blk:
                    emit_ffnA(blk, h, g)
                with nc.named_scope(f"ffnB{h}_{g}"), nc.Block() as blk:
                    emit_hstage(blk, wh1, ha_hi, ha_lo, hb_hi, hb_lo, c_sh1)
                with nc.named_scope(f"ffnC{h}_{g}"), nc.Block() as blk:
                    emit_hstage(blk, wh2, hb_hi, hb_lo, ha_hi, ha_lo, c_sh2)
                with nc.named_scope(f"ffnD{h}_{g}"), nc.Block() as blk:
                    blk_sv = cnt("sv")
                    dadd, dl = emit_ffnD(blk, h, g)
                    if last:
                        def ln2_xmk(c, g2, dadd=dadd, g=g):
                            if g2 == g:
                                return [("sv", dadd[c])]
                            return []
                        norm_ln(blk, xmk=ln2_xmk, entry_sv=blk_sv)

                        @blk.sync
                        def _(sync, h=h):
                            wt(sync, "sv")
                            for c in range(NDC):
                                inc(sync.dma_start(
                                    out=out[c * 128:(c + 1) * 128,
                                            h * H:(h + 1) * H],
                                    in_=X[c][:]), "do", 16)
            if stop_after == "ffn" and h == 0:
                dump_dbg()
                return nc

        with nc.Block() as blk:
            @blk.sync
            def _(sync):
                wt(sync, "do")

    return nc



def _lhsT_slabs(w):
    O, K = w.shape
    a = np.ascontiguousarray(w.T).reshape(K // 128, 128, O // 128, 128)
    a = a.transpose(2, 1, 0, 3)
    return np.ascontiguousarray(a.reshape(O // 128, 128, K))


def _interleave(a, b):
    o = np.empty((a.shape[0] * 2,) + a.shape[1:], dtype=a.dtype)
    o[0::2] = a
    o[1::2] = b
    return np.ascontiguousarray(o)


def _pp(vec):
    return np.ascontiguousarray(np.asarray(vec).reshape(-1, 128).T)


def _make_masks():
    j = np.arange(128)[:, None]
    k = np.arange(256)[None, :]
    valid = (k <= 128 + j) & (k >= j + 1)
    mg = np.where(valid, 0.0, NEG).astype(np.float32)
    m0 = np.where(valid & (k >= 128), 0.0, NEG).astype(np.float32)
    m = np.concatenate([m0, mg], axis=1)
    return np.ascontiguousarray(m.astype(ml_dtypes.bfloat16))


def stage_weights(inputs):
    s = {}
    g3 = _lhsT_slabs(np.asarray(inputs["L0_attn_Wg"]))
    v3 = _lhsT_slabs(np.asarray(inputs["L0_attn_Wv"]))
    d3 = _lhsT_slabs(np.asarray(inputs["L0_attn_Wd"]))
    gvd = np.empty((3 * NDC, 128, D), dtype=np.float32)
    gvd[0::3], gvd[1::3], gvd[2::3] = g3, v3, d3
    s["wgvd"] = np.ascontiguousarray(gvd)
    wqkv = np.asarray(inputs["L0_attn_Wqkv"])
    s["wqk"] = _lhsT_slabs(wqkv[0:2 * D])
    wv = wqkv[2 * D:3 * D]
    s["wvr"] = np.ascontiguousarray(
        np.ascontiguousarray(wv.T).reshape(NDC, 128, D))
    s["wo"] = _lhsT_slabs(np.asarray(inputs["L0_attn_Wo"]))
    s["wgu"] = _interleave(_lhsT_slabs(np.asarray(inputs["L0_attn_Wgate"])),
                           _lhsT_slabs(np.asarray(inputs["L0_attn_Wup"])))
    s["wout"] = _lhsT_slabs(np.asarray(inputs["L0_attn_Wout"]))
    s["w12"] = _interleave(_lhsT_slabs(np.asarray(inputs["L0_ffn_w1_w"])),
                           _lhsT_slabs(np.asarray(inputs["L0_ffn_w2_w"])))
    s["wh1"] = _lhsT_slabs(np.asarray(inputs["L0_ffn_h1_w"]))
    s["wh2"] = _lhsT_slabs(np.asarray(inputs["L0_ffn_h2_w"]))
    s["w3"] = _lhsT_slabs(np.asarray(inputs["L0_ffn_w3_w"]))
    vecs = np.concatenate([
        _pp(inputs["L0_attn_w_rms_mix"]), _pp(inputs["L0_attn_w_rms_attn"]),
        _pp(inputs["L0_attn_w_rms_ffn"]), _pp(inputs["L0_attn_bg"]),
        _pp(inputs["L0_attn_bv"]), _pp(inputs["L0_attn_bd"]),
        _pp(inputs["L0_ffn_w1_s"]), _pp(inputs["L0_ffn_w2_s"]),
        _pp(inputs["L0_ffn_h1_s"]), _pp(inputs["L0_ffn_h2_s"]),
        _pp(inputs["L0_ffn_w3_s"]),
    ], axis=1)
    s["vecs"] = np.ascontiguousarray(vecs.astype(np.float32))
    s["idenf"] = np.eye(128, dtype=np.float32)
    s["idenb"] = np.eye(128, dtype=np.float32).astype(ml_dtypes.bfloat16)
    s["onesf"] = np.ones((128, 128), dtype=np.float32)
    s["masks"] = _make_masks()
    return s


def kernel(**inputs):
    T = 2048
    x = np.asarray(inputs["L0_attn_x"], dtype=np.float32)
    B = x.shape[0]
    shared = stage_weights(inputs)
    nc = build_nc(T)
    in_maps = []
    for b in range(B):
        m = dict(shared)
        m["xt"] = np.ascontiguousarray(x[b].T)
        in_maps.append(m)
    res = run_bass_kernel_spmd(nc, in_maps, core_ids=list(range(B)))
    outs = [np.ascontiguousarray(np.asarray(r["out"]).T) for r in res.results]
    return np.stack(outs).astype(np.float32)
